# version 14
# baseline (speedup 1.0000x reference)
import math
import numpy as np
import ml_dtypes
import concourse.bass as bass
import concourse.mybir as mybir
from concourse.bass_utils import run_bass_kernel_spmd

F32 = mybir.dt.float32
BF16 = mybir.dt.bfloat16
AF = mybir.ActivationFunctionType
ALU = mybir.AluOpType
AX = mybir.AxisListType

D = 1024
DC = 8
E = 16
FP = 256
DFF = 4096


def _host_consts(T):
    half = D // 2
    freqs = np.exp(-np.arange(half, dtype=np.float64) * (math.log(10000.0) / half))
    times = np.arange(T, dtype=np.float32).astype(np.float64)
    th = (times[:, None].astype(np.float32) * freqs[None, :].astype(np.float32)).astype(np.float32)
    cs = np.empty((T, D), dtype=np.float32)
    cs[:, :half] = np.cos(th.astype(np.float64)).astype(np.float32)
    cs[:, half:] = np.sin(th.astype(np.float64)).astype(np.float32)

    LT = np.triu(np.ones((128, 128), dtype=np.float32))
    idf = np.eye(128, dtype=np.float32)
    cf32 = np.concatenate([LT, idf, np.ones((128, 1), np.float32)], axis=1)

    kk = np.arange(128)[:, None]
    qq = np.arange(256)[None, :]
    m0 = (kk <= qq).astype(ml_dtypes.bfloat16)
    m1 = (kk <= qq - 128).astype(ml_dtypes.bfloat16)
    cbf = np.concatenate([m0, m1, np.ones((128, 1), ml_dtypes.bfloat16)], axis=1)
    return cs, cf32, cbf


def _pack_col(vec):
    C = vec.shape[0] // 128
    return np.ascontiguousarray(vec.reshape(C, 128).T)


def _host_inputs(inputs, T):
    g = lambda k: np.asarray(inputs[k], dtype=np.float32)

    def flatw(w):
        Kd, Nd = w.shape
        return np.ascontiguousarray(
            w.reshape(Kd // 128, 128, Nd).transpose(1, 0, 2).reshape(128, -1))

    wqkv = np.concatenate([flatw(g("L0_attn_Wq")), flatw(g("L0_attn_Wk")),
                           flatw(g("L0_attn_Wv"))], axis=1)
    wgp = np.concatenate([flatw(g("L0_attn_gate_W")), flatw(g("L0_attn_p_W"))],
                         axis=1)
    wmisc = np.ascontiguousarray(
        np.concatenate([g("L0_attn_clock_W").reshape(DC, 128, 1),
                        g("L0_ffn_wg").reshape(DC, 128, E)], axis=2)
        .transpose(1, 0, 2).reshape(128, DC * 17))
    wout = flatw(g("L0_attn_out_W"))
    wmlp1 = flatw(g("L0_attn_mlp_W1"))
    wmlp2 = flatw(g("L0_attn_mlp_W2"))
    w1cat = np.ascontiguousarray(g("L0_ffn_w1").transpose(1, 0, 2).reshape(D, E * FP))
    wmoe1 = flatw(w1cat)
    wmoe2 = flatw(np.ascontiguousarray(g("L0_ffn_w2").reshape(E * FP, D)))
    b2cat = g("L0_ffn_b2")

    ppack = np.concatenate([
        _pack_col(g("L0_attn_ln1_g")), _pack_col(g("L0_attn_ln1_b")),
        _pack_col(g("L0_attn_ln2_g")), _pack_col(g("L0_attn_ln2_b")),
        _pack_col(g("L0_ffn_gamma")), _pack_col(g("L0_ffn_beta")),
        _pack_col(g("L0_attn_out_b")), _pack_col(g("L0_attn_mlp_b2")),
        _pack_col(g("L0_attn_mlp_b1")), _pack_col(g("L0_ffn_b1").reshape(-1)),
    ], axis=1)

    prow = np.zeros((1, 2072), dtype=np.float32)
    prow[0, 0:1024] = g("L0_attn_gate_b")
    prow[0, 1024:2048] = g("L0_attn_p_b")
    prow[0, 2048:2064] = g("L0_ffn_bg")
    prow[0, 2064] = g("L0_attn_clock_b")[0]
    prow[0, 2065] = 1e-5
    prow[0, 2066] = 1e-8
    prow[0, 2067] = 1.0
    prow[0, 2068] = float(T // 2)

    cs, cf32, cbf = _host_consts(T)
    return dict(wqkv=wqkv, wgp=wgp, wmisc=wmisc, wout=wout, wmlp1=wmlp1,
                wmlp2=wmlp2, wmoe1=wmoe1, wmoe2=wmoe2, b2cat=b2cat,
                ppack=ppack, prow=prow, cs_tab=cs, cf32=cf32, cbf16=cbf)



ENGINES = ("tensor", "vector", "scalar", "gpsimd", "sync")


class Em:

    def __init__(self, nc):
        self.nc = nc
        self.ops = {e: [] for e in ENGINES}
        self.sem = {}
        self.cnt = {}
        self.hwm = {e: {} for e in ENGINES}
        self._es = None

    def add_sem(self, name, handle):
        self.sem[name] = handle
        self.cnt[name] = 0

    def op(self, eng, fn, wait=(), sig=None, dma=False, osz=None):
        tok = None
        if sig is not None:
            self.cnt[sig] += 16 if dma else 1
            tok = (sig, self.cnt[sig])
        for (s, v) in wait:
            assert v <= self.cnt[s], f"wait on future token {s}={v} (now {self.cnt[s]})"
        self.ops[eng].append((tuple(wait), fn,
                              (sig, 16 if dma else 1) if sig else None, osz))
        return tok

    @staticmethod
    def _osz(out):
        try:
            sh = out.shape
        except Exception:
            return None
        n = 1
        for d in list(sh)[1:]:
            n *= int(d)
        return n

    def t(self, fn, wait=(), sig="te"):
        return self.op("tensor", fn, wait, sig)

    def v(self, fn, wait=(), sig="ve"):
        return self.op("vector", fn, wait, sig)

    def s(self, fn, wait=(), sig="sc"):
        return self.op("scalar", fn, wait, sig)

    def g(self, fn, wait=(), sig="gp"):
        return self.op("gpsimd", fn, wait, sig)

    def dma(self, queue, out, in_, wait=(), sem=None):
        semname = sem or {"sync": "qs0", "scalar": "qa0", "gpsimd": "qg"}[queue]
        return self.op(queue, lambda e: e.dma_start(out=out, in_=in_),
                       wait, semname, dma=True)

    def barrier(self):
        snap = {s: self.cnt[s] for s in self.cnt}
        w = [(s, v) for s, v in snap.items() if v > 0]
        for e in ENGINES:
            self.op(e, lambda _e: None, wait=w)
        return snap

    def mm(self, out, lhsT, rhs, start, stop, wait=(), sig="te"):
        return self.op("tensor", lambda e, o=out, l=lhsT, r=rhs, a=start, b=stop:
                       e.matmul(o, l, r, start=a, stop=b), wait, sig)

    def transp(self, out, in_, ident, wait=(), sig="te"):
        return self.op("tensor", lambda e, o=out, i=in_, d=ident: e.transpose(o, i, d),
                       wait, sig)

    def tt(self, eng, out, a, b, op, wait=(), sig=None):
        sig = sig or {"vector": "ve", "gpsimd": "gp"}[eng]
        return self.op(eng, lambda e, o=out, x=a, y=b, p=op: e.tensor_tensor(o, x, y, p),
                       wait, sig, osz=self._osz(out))

    def ts(self, eng, out, in0, s1, s2, op0, op1=None, accum=None, wait=(), sig=None):
        sig = sig or {"vector": "ve", "gpsimd": "gp"}[eng]
        if op1 is None:
            return self.op(eng, lambda e, o=out, i=in0, a=s1, p=op0:
                           e.tensor_single_scalar(o, i, a, p), wait, sig,
                           osz=self._osz(out))
        return self.op(eng, lambda e, o=out, i=in0, a=s1, b=s2, p=op0, q=op1, ac=accum:
                       e.tensor_scalar(o, i, a, b, p, q, accum_out=ac), wait, sig,
                       osz=self._osz(out))

    def stt(self, eng, out, in0, scalar, in1, op0, op1, wait=(), sig=None):
        sig = sig or {"vector": "ve", "gpsimd": "gp"}[eng]
        return self.op(eng, lambda e, o=out, i=in0, s=scalar, j=in1, p=op0, q=op1:
                       e.scalar_tensor_tensor(o, i, s, j, p, q), wait, sig,
                       osz=self._osz(out))

    def act(self, out, in_, func, bias=0.0, scale=1.0, accum=None, wait=(), sig="sc"):
        oz = 1 if accum is not None else self._osz(out)
        return self.op("scalar",
                       lambda e, o=out, i=in_, f=func, b=bias, sc=scale, ac=accum:
                       e.activation(o, i, f, bias=b, scale=sc, accum_out=ac),
                       wait, sig, osz=oz)

    def ttr(self, out, in0, in1, scale, accum, wait=(), sig="ve"):
        return self.op("vector",
                       lambda e, o=out, a=in0, b=in1, s=scale, ac=accum:
                       e.tensor_tensor_reduce(o, a, b, s, 0.0, ALU.mult, ALU.add,
                                              accum_out=ac),
                       wait, sig, osz=1)

    def red(self, out, in_, op, wait=(), sig="ve", negate=None):
        return self.op("vector", lambda e, o=out, i=in_, p=op, n=negate:
                       e.tensor_reduce(o, i, AX.X, p, negate=n), wait, sig,
                       osz=self._osz(out))

    def cp(self, eng, out, in_, wait=(), sig=None):
        sig = sig or {"vector": "ve", "gpsimd": "gp", "scalar": "sc"}[eng]
        if eng == "scalar":
            return self.op(eng, lambda e, o=out, i=in_: e.copy(o, i), wait, sig,
                           osz=self._osz(out))
        return self.op(eng, lambda e, o=out, i=in_: e.tensor_copy(o, i), wait, sig,
                       osz=self._osz(out))

    def build(self, block):
        own_sem = {"vector": "ve", "scalar": "sc"}

        def mk(eng_name):
            osem = own_sem.get(eng_name)

            def run(eng):
                hwm = {}
                cnt = 0
                prev_small = False
                for waits, fn, inc, osz in self.ops[eng_name]:
                    merged = {}
                    for (s, v) in waits:
                        if v > merged.get(s, 0):
                            merged[s] = v
                    if osem is not None and prev_small and cnt > merged.get(osem, 0):
                        merged[osem] = cnt
                    for s, v in merged.items():
                        if hwm.get(s, 0) < v:
                            eng.wait_ge(self.sem[s], v)
                            hwm[s] = v
                    ins = fn(eng)
                    if inc is not None:
                        assert ins is not None, f"sig op returned None on {eng_name}"
                        ins.then_inc(self.sem[inc[0]], inc[1])
                        if inc[0] == osem:
                            cnt += inc[1]
                    prev_small = (osz is not None and osz < 256)
            return run
        block.tensor(mk("tensor"))
        block.vector(mk("vector"))
        block.scalar(mk("scalar"))
        block.gpsimd(mk("gpsimd"))
        block.sync(mk("sync"))


class Carver:

    def __init__(self, arena_ap, nbytes):
        self.ap = arena_ap
        self.nbytes = nbytes
        self.off = 0

    def take(self, n_elems, dtype):
        esz = 4 if dtype == F32 else 2
        start = (self.off + 63) & ~63
        nb = n_elems * esz
        assert start + nb <= self.nbytes, f"SBUF arena overflow: {start + nb} > {self.nbytes}"
        self.off = start + nb
        sl = self.ap[:, start // 2:(start + nb) // 2]
        if dtype == F32:
            sl = sl.bitcast(F32)
        return sl

    def f32(self, n):
        return self.take(n, F32)

    def bf16(self, n):
        return self.take(n, BF16)

    def reset(self):
        self.off = 0



_PHASES = 5

def build_nc(T=8192, taps=()):
    NTT = T // 128
    nc = bass.Bass("TRN2", target_bir_lowering=False, debug=False)

    x_h = nc.dram_tensor("x", [T, D], F32, kind="ExternalInput").ap()
    wqkv_h = nc.dram_tensor("wqkv", [128, 3 * DC * D], F32, kind="ExternalInput").ap()
    wgp_h = nc.dram_tensor("wgp", [128, 2 * DC * D], F32, kind="ExternalInput").ap()
    wmisc_h = nc.dram_tensor("wmisc", [128, DC * 17], F32, kind="ExternalInput").ap()
    wout_h = nc.dram_tensor("wout", [128, DC * D], F32, kind="ExternalInput").ap()
    wmlp1_h = nc.dram_tensor("wmlp1", [128, DC * DFF], F32, kind="ExternalInput").ap()
    wmlp2_h = nc.dram_tensor("wmlp2", [128, 32 * D], F32, kind="ExternalInput").ap()
    wmoe1_h = nc.dram_tensor("wmoe1", [128, DC * DFF], F32, kind="ExternalInput").ap()
    wmoe2_h = nc.dram_tensor("wmoe2", [128, 32 * D], F32, kind="ExternalInput").ap()
    b2cat_h = nc.dram_tensor("b2cat", [E, D], F32, kind="ExternalInput").ap()
    ppack_h = nc.dram_tensor("ppack", [128, 128], F32, kind="ExternalInput").ap()
    prow_h = nc.dram_tensor("prow", [1, 2072], F32, kind="ExternalInput").ap()
    cs_h = nc.dram_tensor("cs_tab", [T, D], F32, kind="ExternalInput").ap()
    cf32_h = nc.dram_tensor("cf32", [128, 257], F32, kind="ExternalInput").ap()
    cbf_h = nc.dram_tensor("cbf16", [128, 513], BF16, kind="ExternalInput").ap()

    def scr(name, shape, dtype):
        kind = "ExternalOutput" if name in taps else "Internal"
        return nc.dram_tensor(name, shape, dtype, kind=kind).ap()

    xnT_h = scr("xnT", [NTT // 2, 128, 2 * D], BF16)
    xn2T_h = scr("xn2T", [NTT, 128, D], BF16)
    tclip_h = scr("tclip", [NTT, 128, D], F32)
    pnat_h = scr("pnat", [NTT, 128, D], F32)
    qT_h = scr("qT", [NTT // 2, 128, 2 * D], BF16)
    kT_h = scr("kT", [NTT, 128, D], BF16)
    qraw_h = scr("qraw", [NTT, 128, D], BF16)
    kraw_h = scr("kraw", [NTT, 128, D], BF16)
    vnat_h = scr("vnat", [NTT, 128, D], BF16)
    x2T_h = scr("x2T", [NTT // 2, 128, 2 * D], BF16)
    gT_h = scr("gT", [NTT // 2, 128, 2 * D], BF16)
    selT_h = scr("selT", [E, T], BF16)
    pmaxrow_h = scr("pmaxrow", [1, D], F32)
    thrrow_h = scr("thrrow", [1, E], F32)
    dbg0_h = scr("dbg0", [128, 4 * 64 * 16], F32)
    dbg1_h = scr("dbg1", [NTT, 128, 16], F32)
    dbg2_h = scr("dbg2", [NTT, 128, D], F32)
    out_h = nc.dram_tensor("out", [T, D], F32, kind="ExternalOutput").ap()

    import contextlib
    es = contextlib.ExitStack()
    with es:
        ARENA_BYTES = 196 * 1024
        AR = es.enter_context(nc.sbuf_tensor("AR", [128, ARENA_BYTES // 2], BF16))
        PCtens = es.enter_context(nc.sbuf_tensor("PC", [128, 2048], F32))
        PBtens = es.enter_context(nc.sbuf_tensor("PB", [128, 1024], BF16))
        pb = [es.enter_context(nc.psum_tensor(f"pb{i}", [128, 512], F32)) for i in range(8)]
        em = Em(nc)
        for nm in ("te", "ve", "sc", "gp", "qs0", "qs1", "qs2", "qs3",
                   "qa0", "qa1", "qg", "qc", "qk0", "qk1", "qk2", "qk3",
                   "qb0", "qb1", "qv0", "qv1"):
            em.add_sem(nm, es.enter_context(nc.semaphore(nm)))
        block = es.enter_context(nc.Block())

        pcv = Carver(PCtens.ap().bitcast(BF16), 8192)
        ppack = pcv.f32(128)
        clock_all = pcv.f32(NTT)
        pmax_acc = pcv.f32(D)
        bg_bc = pcv.f32(16)
        cb_bc = pcv.f32(1)
        eps5 = pcv.f32(1)
        eps8 = pcv.f32(1)
        onet = pcv.f32(1)
        cf32 = pcv.f32(257)
        pbv = Carver(PBtens.ap(), 2048)
        cbf = pbv.bf16(513)

        LT = cf32[:, 0:128]
        IDF = cf32[:, 128:256]
        ONESC = cf32[:, 256:257]
        ONESR = cf32[0:1, 0:128]
        ONESC_B = cbf[:, 512:513]
        ONESR_B = cbf[0:1, 0:128]

        def pcol(base, n):
            return [ppack[:, base + c:base + c + 1] for c in range(n)]
        ln1g_c, ln1b_c = pcol(0, 8), pcol(8, 8)
        ln2g_c, ln2b_c = pcol(16, 8), pcol(24, 8)
        gam_c, bet_c = pcol(32, 8), pcol(40, 8)
        outb_c, mlpb2_c = pcol(48, 8), pcol(56, 8)
        mlpb1_c, b1cat_c = pcol(64, 32), pcol(96, 32)

        tfix = []
        tfix.append(em.dma("sync", ppack, ppack_h, sem="qc"))
        tfix.append(em.dma("sync", cf32, cf32_h, sem="qc"))
        tfix.append(em.dma("sync", cbf, cbf_h, sem="qc"))
        tfix.append(em.dma("sync", bg_bc, prow_h[:, 2048:2064].broadcast_to([128, 16]), sem="qc"))
        tfix.append(em.dma("sync", cb_bc, prow_h[:, 2064:2065].broadcast_to([128, 1]), sem="qc"))
        tfix.append(em.dma("sync", eps5, prow_h[:, 2065:2066].broadcast_to([128, 1]), sem="qc"))
        tfix.append(em.dma("sync", eps8, prow_h[:, 2066:2067].broadcast_to([128, 1]), sem="qc"))
        tfix.append(em.dma("sync", onet, prow_h[:, 2067:2068].broadcast_to([128, 1]), sem="qc"))

        ctx = dict(locals())
        ctx["taps"] = taps

        _phase0(ctx, Carver(AR.ap(), ARENA_BYTES))
        em.barrier()
        _phase1(ctx, Carver(AR.ap(), ARENA_BYTES))
        em.barrier()
        if _PHASES >= 3:
            _phase2(ctx, Carver(AR.ap(), ARENA_BYTES))
            em.barrier()
        if _PHASES >= 4:
            _phase3(ctx, Carver(AR.ap(), ARENA_BYTES))
            em.barrier()
        if _PHASES >= 5:
            _phase4(ctx, Carver(AR.ap(), ARENA_BYTES))

        snap = {s: em.cnt[s] for s in em.cnt}
        w = [(s, v) for s, v in snap.items() if v > 0]
        em.op("sync", lambda e: None, wait=w)
        em.build(block)
    return nc


def _phase0(c, cv):
    nc, em, pb, T, NTT = c["nc"], c["em"], c["pb"], c["T"], c["NTT"]
    LT, IDF, ONESC, ONESR = c["LT"], c["IDF"], c["ONESC"], c["ONESR"]
    eps5, eps8, onet = c["eps5"], c["eps8"], c["onet"]
    x_h, wgp_h, wmisc_h = c["x_h"], c["wgp_h"], c["wmisc_h"]
    xnT_h, xn2T_h, tclip_h, pnat_h = c["xnT_h"], c["xn2T_h"], c["tclip_h"], c["pnat_h"]
    selT_h, pmaxrow_h, thrrow_h, prow_h = c["selT_h"], c["pmaxrow_h"], c["thrrow_h"], c["prow_h"]
    ln1g_c, ln1b_c = c["ln1g_c"], c["ln1b_c"]
    clock_all, bg_bc, cb_bc = c["clock_all"], c["bg_bc"], c["cb_bc"]

    xf = [cv.f32(D) for _ in range(3)]
    xn = [cv.f32(D), cv.f32(D)]
    xn2f = [cv.f32(D), cv.f32(D)]
    sqd = cv.f32(D)
    pmax_acc = c["pmax_acc"]
    glf = [cv.f32(D)]
    ta = [cv.f32(D)]
    tb = [cv.f32(D)]
    spf = [cv.f32(D), cv.f32(D)]
    tclipf = [cv.f32(D), cv.f32(D)]
    pnf = [cv.f32(D), cv.f32(D)]
    carry = cv.f32(D)[0:1, :]
    czcarry = cv.f32(16)[0:1, :]
    lgf = [cv.f32(16), cv.f32(16)]
    mk1, mk2, lg2 = cv.f32(16), cv.f32(16), cv.f32(16)
    sm = [cv.f32(16) for _ in range(4)]
    mask_all = cv.f32(NTT * 16)
    cumz_all = cv.f32(NTT * 16)
    ind_all = cumz_all
    zif = [cv.f32(16), cv.f32(16)]
    thr_bc = cv.f32(16)
    pmr = cv.f32(16)
    pmst = cv.f32(128)
    thrr = cv.f32(16)[0:1, :]
    stgA = [cv.bf16(D), cv.bf16(D)]
    stgB = [cv.bf16(D), cv.bf16(D)]
    selTst = cv.bf16(T)
    wgp_sb = cv.bf16(2 * DC * D)
    wmisc_sb = cv.bf16(DC * 17)
    qkv_sb = cv.bf16(3 * DC * D)
    stgQR = cv.bf16(D)
    stgKR = cv.bf16(D)
    stgVR = cv.bf16(D)
    gbrow = cv.bf16(D)[0:1, :]
    pbrow = cv.bf16(D)[0:1, :]

    twg = em.dma("gpsimd", wgp_sb, wgp_h)
    twm = em.dma("gpsimd", wmisc_sb, wmisc_h)
    twq0 = em.dma("gpsimd", qkv_sb, c["wqkv_h"])
    tgrow = em.dma("gpsimd", gbrow, prow_h[:, 0:1024])
    tprow = em.dma("gpsimd", pbrow, prow_h[:, 1024:2048])
    tmc = em.v(lambda e: e.memset(carry, 0.0))
    tmz = em.v(lambda e: e.memset(czcarry, 0.0))
    wconsts = c["tfix"] + [twg, twm, twq0, tgrow, tprow, tmc, tmz]
    for eng in ENGINES:
        em.op(eng, lambda e: None, wait=wconsts)

    xf_free = [None] * 3
    xn_lastT = [None, None]
    xn2_lastT = [None, None]
    stg_free = [[], []]
    glf_free = [None]
    sp_free = [None, None]
    pn_free = [[], []]
    tclip_free = [None, None]
    lgf_free = [None, None]
    bank_free = [None] * 8
    qkvst_free = [None, None]
    zif_free = [None, None]
    tsp_prev = None
    t_carryupd = None
    t_czupd = None
    t_pmax = None

    x_t = x_h.rearrange("(n p) d -> n p d", p=128)
    htok = {}

    def head(i):
        sl = i % 2
        fl = i % 3
        S = sm[i % 4]
        s0, s1, s2, s3 = S[:, 0:1], S[:, 1:2], S[:, 2:3], S[:, 3:4]
        s4, s5, s6 = S[:, 4:5], S[:, 5:6], S[:, 6:7]
        s7, s8, s9 = S[:, 7:8], S[:, 8:9], S[:, 9:10]
        s10, s11, s12, s13 = S[:, 10:11], S[:, 11:12], S[:, 12:13], S[:, 13:14]
        tL = em.dma("sync", xf[fl], x_t[i], sem=f"qs{fl}",
                    wait=[xf_free[fl]] if xf_free[fl] else [])
        tsum = em.red(s0, xf[fl], ALU.add, wait=[tL])
        tsq = em.act(sqd, xf[fl], AF.Square, accum=s2, wait=[tL])
        em.ts("vector", s1, s0, 1.0 / D, None, ALU.mult)
        em.ts("vector", s3, s2, 1.0 / D, None, ALU.mult, wait=[tsq])
        em.tt("vector", s4, s1, s1, ALU.mult)
        tvar = em.tt("vector", s4, s3, s4, ALU.subtract)
        tln = em.act(s5, s4, AF.Ln, bias=eps5, wait=[tvar])
        trstd = em.act(s6, s5, AF.Exp, scale=-0.5, wait=[tln])
        wxn = [trstd] + ([xn_lastT[sl]] if xn_lastT[sl] else [])
        txn = em.ts("vector", xn[sl], xf[fl], s1, s6, ALU.subtract, ALU.mult, wait=wxn)
        xf_free[fl] = txn
        tsum1 = em.red(s7, xn[sl], ALU.add, wait=[txn])
        tsq1 = em.act(sqd, xn[sl], AF.Square, accum=s8, wait=[txn])
        em.ts("vector", s9, s7, 1.0 / D, None, ALU.mult)
        em.ts("vector", s10, s8, 1.0 / D, None, ALU.mult, wait=[tsq1])
        em.tt("vector", s11, s9, s9, ALU.mult)
        tvar1 = em.tt("vector", s11, s10, s11, ALU.subtract)
        tln1 = em.act(s12, s11, AF.Ln, bias=eps5, wait=[tvar1])
        trstd1 = em.act(s13, s12, AF.Exp, scale=-0.5, wait=[tln1])
        wxn2 = [trstd1] + ([xn2_lastT[sl]] if xn2_lastT[sl] else [])
        txn2 = em.ts("vector", xn2f[sl], xn[sl], s9, s13, ALU.subtract, ALU.mult,
                     wait=wxn2)
        htok[i] = (txn, txn2)

    def tail(i):
        nonlocal t_carryupd, t_czupd, t_pmax, tsp_prev
        sl = i % 2
        S = sm[i % 4]
        s0, s2 = S[:, 0:1], S[:, 2:3]
        txn, txn2 = htok.pop(i)

        tA = tB = None
        tt1_last = tt2_last = None
        evA = {}
        evB = {}
        for dc in range(DC):
            b0 = dc % 2
            b1 = 2 + dc % 2
            wA = [txn] + (stg_free[sl] if dc == 0 else [])
            if dc >= 2:
                wA.append(evA[dc - 2])
            elif bank_free[b0]:
                wA.append(bank_free[b0])
            tt1 = em.transp(pb[b0][:, 0:128], xn[sl][:, dc * 128:(dc + 1) * 128],
                            IDF, wait=wA)
            tA = em.cp("scalar", stgA[sl][:, dc * 128:(dc + 1) * 128],
                       pb[b0][:, 0:128], wait=[tt1])
            evA[dc] = tA
            bank_free[b0] = tA
            wB = [txn2]
            if dc >= 2:
                wB.append(evB[dc - 2])
            elif bank_free[b1]:
                wB.append(bank_free[b1])
            tt2 = em.transp(pb[b1][:, 0:128], xn2f[sl][:, dc * 128:(dc + 1) * 128],
                            IDF, wait=wB)
            tB = em.ts("vector", stgB[sl][:, dc * 128:(dc + 1) * 128],
                       pb[b1][:, 0:128], ln1g_c[dc], ln1b_c[dc],
                       ALU.mult, ALU.add, wait=[tt2])
            evB[dc] = tB
            bank_free[b1] = tB
            tt1_last, tt2_last = tt1, tt2
        xn_lastT[sl] = tt1_last
        xn2_lastT[sl] = tt2_last
        tdA = em.dma("scalar",
                     xnT_h[i // 2].rearrange("p (c u n) -> p c u n", c=DC, u=2)[:, :, i % 2, :],
                     stgA[sl].rearrange("p (c n) -> p c n", c=DC),
                     wait=[tA], sem=f"qa{sl}")

        wQ = [tB] + [t for t in (bank_free[4], bank_free[5]) if t]
        wK = [tB] + [t for t in (bank_free[6], bank_free[7]) if t]
        wV = [tB, evB[6], evB[7]]
        tq_mm = tk_mm = tv_mm = None
        for dc in range(DC):
            lhsT = stgB[sl][:, dc * 128:(dc + 1) * 128]
            st_, sp_ = (dc == 0), (dc == DC - 1)
            em.mm(pb[4][:, :], lhsT, qkv_sb[:, dc * 1024:dc * 1024 + 512],
                  st_, sp_, wait=wQ if dc == 0 else [])
            tq_mm = em.mm(pb[5][:, :], lhsT,
                          qkv_sb[:, dc * 1024 + 512:(dc + 1) * 1024], st_, sp_)
            em.mm(pb[6][:, :], lhsT,
                  qkv_sb[:, 8192 + dc * 1024:8192 + dc * 1024 + 512],
                  st_, sp_, wait=wK if dc == 0 else [])
            tk_mm = em.mm(pb[7][:, :], lhsT,
                          qkv_sb[:, 8192 + dc * 1024 + 512:8192 + (dc + 1) * 1024],
                          st_, sp_)
            em.mm(pb[2][:, :], lhsT,
                  qkv_sb[:, 16384 + dc * 1024:16384 + dc * 1024 + 512],
                  st_, sp_, wait=wV if dc == 0 else [])
            tv_mm = em.mm(pb[3][:, :], lhsT,
                          qkv_sb[:, 16384 + dc * 1024 + 512:16384 + (dc + 1) * 1024],
                          st_, sp_)
        wsq = [tq_mm] + ([qkvst_free[sl]] if qkvst_free[sl] else [])
        em.cp("scalar", stgQR[:, 0:512], pb[4][:, :], wait=wsq)
        teq = em.cp("scalar", stgQR[:, 512:1024], pb[5][:, :])
        em.cp("scalar", stgKR[:, 0:512], pb[6][:, :], wait=[tk_mm])
        tek = em.cp("scalar", stgKR[:, 512:1024], pb[7][:, :])
        em.cp("scalar", stgVR[:, 0:512], pb[2][:, :], wait=[tv_mm])
        tev = em.cp("scalar", stgVR[:, 512:1024], pb[3][:, :])
        em.dma("scalar", c["qraw_h"][i], stgQR, wait=[teq], sem=f"qv{sl}")
        em.dma("scalar", c["kraw_h"][i], stgKR, wait=[tek], sem=f"qv{sl}")
        tqkvd = em.dma("scalar", c["vnat_h"][i], stgVR, wait=[tev], sem=f"qv{sl}")
        qkvst_free[sl] = tqkvd
        stg_free[sl] = [tv_mm, tdA]

        wg_ = [tB, teq]
        wp_ = [tB, tek]
        wck = [tB, evA[6]]

        wlg = [tA, evA[7]]
        tgA = tgB_ = tpA = tpB_ = tckm = tlgm = None
        for dc in range(DC):
            lhsT = stgB[sl][:, dc * 128:(dc + 1) * 128]
            lhsTA = stgA[sl][:, dc * 128:(dc + 1) * 128]
            st_, sp_ = (dc == 0), (dc == DC - 1)
            tgA = em.mm(pb[4][:, :], lhsT, wgp_sb[:, dc * 1024:dc * 1024 + 512],
                        st_, False, wait=wg_ if dc == 0 else [])
            tgB_ = em.mm(pb[5][:, :], lhsT, wgp_sb[:, dc * 1024 + 512:(dc + 1) * 1024],
                         st_, False)
            tpA = em.mm(pb[6][:, :], lhsT,
                        wgp_sb[:, 8192 + dc * 1024:8192 + dc * 1024 + 512],
                        st_, False, wait=wp_ if dc == 0 else [])
            tpB_ = em.mm(pb[7][:, :], lhsT,
                         wgp_sb[:, 8192 + dc * 1024 + 512:8192 + (dc + 1) * 1024],
                         st_, False)
            tckm = em.mm(pb[0][:, 0:1], lhsT, wmisc_sb[:, dc * 17:dc * 17 + 1],
                         st_, sp_, wait=wck if dc == 0 else [])
            tlgm = em.mm(pb[1][:, 0:16], lhsTA, wmisc_sb[:, dc * 17 + 1:(dc + 1) * 17],
                         st_, sp_, wait=wlg if dc == 0 else [])


        stg_free[sl] = [tlgm, tdA]

        ONESR_B = c["cbf"][0:1, 0:128]
        em.mm(pb[4][:, :], ONESR_B, gbrow[:, 0:512], False, True)
        tgB_ = em.mm(pb[5][:, :], ONESR_B, gbrow[:, 512:1024], False, True)
        em.mm(pb[6][:, :], ONESR_B, pbrow[:, 0:512], False, True)
        tpB_ = em.mm(pb[7][:, :], ONESR_B, pbrow[:, 512:1024], False, True)

        wg0 = [tgB_] + ([glf_free[0]] if glf_free[0] else [])
        em.cp("vector", glf[0][:, 0:512], pb[4][:, :], wait=wg0)
        tgl = em.cp("vector", glf[0][:, 512:1024], pb[5][:, :])
        wsp = [tgl] + ([sp_free[sl]] if sp_free[sl] else []) \
            + ([tsp_prev] if tsp_prev else [])
        tab = em.act(ta[0], glf[0], AF.Abs, wait=wsp)
        tex = em.act(tb[0], ta[0], AF.Exp, scale=-1.0, wait=[tab])
        tlnsp = em.act(ta[0], tb[0], AF.Ln, bias=onet, wait=[tex])
        trel = em.ts("vector", tb[0], glf[0], 0.0, None, ALU.max, wait=[tlnsp])
        glf_free[0] = trel
        tsp = em.tt("vector", spf[sl], ta[0], tb[0], ALU.add, wait=[tlnsp, trel])
        tsp_prev = tsp

        wp0 = [tpB_] + pn_free[sl]
        em.cp("vector", pnf[sl][:, 0:512], pb[6][:, :], wait=wp0)
        tpn = em.cp("vector", pnf[sl][:, 512:1024], pb[7][:, :])
        tpd = em.dma("scalar", pnat_h[i], pnf[sl], wait=[tpn], sem=f"qb{sl}")
        del tpd
        if i == 0:
            t_pmax = em.cp("vector", pmax_acc, pnf[sl], wait=[tpn])
        else:
            t_pmax = em.tt("vector", pmax_acc, pmax_acc, pnf[sl], ALU.max,
                           wait=[tpn, t_pmax])

        cl0, cl1 = S[:, 14:15], S[:, 15:16]
        tc0 = em.tt("vector", cl0, pb[0][:, 0:1], cb_bc, ALU.add, wait=[tckm])
        bank_free[0] = tc0
        tc1 = em.act(cl1, cl0, AF.Abs, wait=[tc0])
        tc2 = em.act(cl1, cl1, AF.Exp, scale=-1.0, wait=[tc1])
        tc3 = em.act(cl1, cl1, AF.Ln, bias=onet, wait=[tc2])
        tc4 = em.ts("vector", cl0, cl0, 0.0, None, ALU.max, wait=[tc3])
        em.tt("vector", clock_all[:, i:i + 1], cl0, cl1, ALU.add, wait=[tc4])

        wl0 = [tlgm] + ([lgf_free[sl]] if lgf_free[sl] else [])
        tlg0 = em.tt("vector", lgf[sl], pb[1][:, 0:16], bg_bc, ALU.add, wait=wl0)
        bank_free[1] = tlg0
        trm = em.red(s0, lgf[sl], ALU.max, wait=[tlg0])
        tm1 = em.ts("vector", mk1, lgf[sl], s0, None, ALU.is_ge, wait=[trm])
        tl2 = em.stt("vector", lg2, mk1, -1e30, lgf[sl], ALU.mult, ALU.add, wait=[tm1])
        lgf_free[sl] = tl2
        trm2 = em.red(s2, lg2, ALU.max, wait=[tl2])
        tm2 = em.ts("vector", mk2, lg2, s2, None, ALU.is_ge, wait=[trm2])
        msk_sl = mask_all[:, i * 16:(i + 1) * 16]
        tmask = em.tt("vector", msk_sl, mk1, mk2, ALU.add, wait=[tm2])
        zi_sl = zif[sl]
        wzi = [tmask] + ([zif_free[sl]] if zif_free[sl] else [])
        tzi = em.ts("vector", zi_sl, msk_sl, -1.0, 1.0, ALU.mult, ALU.add, wait=wzi)

        wc = [tsp, tpn] + ([t_carryupd] if t_carryupd else [])
        em.mm(pb[6][:, :], ONESR, carry[:, 0:512], True, False, wait=wc)
        em.mm(pb[7][:, :], ONESR, carry[:, 512:1024], True, False)
        em.mm(pb[6][:, :], LT, spf[sl][:, 0:512], False, True)
        tcs = em.mm(pb[7][:, :], LT, spf[sl][:, 512:1024], False, True)
        wcs = [tsp, tev]
        em.mm(pb[2][0:1, 0:512], ONESC, spf[sl][:, 0:512], True, True, wait=wcs)
        tcol = em.mm(pb[3][0:1, 0:512], ONESC, spf[sl][:, 512:1024], True, True)
        wtc = [tcs] + ([tclip_free[sl]] if tclip_free[sl] else [])
        em.ts("vector", tclipf[sl][:, 0:512], pb[6][:, :], 60.0, None, ALU.min,
              wait=wtc)
        ttc = em.ts("vector", tclipf[sl][:, 512:1024], pb[7][:, :], 60.0, None, ALU.min)
        bank_free[6] = ttc
        bank_free[7] = ttc
        ttd = em.dma("scalar", tclip_h[i], tclipf[sl], wait=[ttc], sem=f"qb{sl}")
        tclip_free[sl] = ttd
        pn_free[sl] = [t_pmax, ttd]
        em.tt("vector", carry[:, 0:512], carry[:, 0:512], pb[2][0:1, 0:512], ALU.add,
              wait=[tcol])
        t_carryupd = em.tt("vector", carry[:, 512:1024], carry[:, 512:1024],
                           pb[3][0:1, 0:512], ALU.add)
        bank_free[2] = t_carryupd
        bank_free[3] = t_carryupd

        wz = [tzi, tgl] + ([t_czupd] if t_czupd else [])
        em.mm(pb[4][:, 0:16], ONESR, czcarry, True, False, wait=wz)
        tcz = em.mm(pb[4][:, 0:16], LT, zi_sl, False, True)
        tczs = em.mm(pb[5][0:1, 0:16], ONESC, zi_sl, True, True)
        tcza = em.cp("vector", cumz_all[:, i * 16:(i + 1) * 16], pb[4][:, 0:16],
                     wait=[tcz])
        zif_free[sl] = tczs
        t_czupd = em.tt("vector", czcarry, czcarry, pb[5][0:1, 0:16], ALU.add,
                        wait=[tczs])
        bank_free[4] = tcza
        bank_free[5] = t_czupd

    head(0)
    for i in range(NTT):
        if i + 1 < NTT:
            head(i + 1)
        tail(i)

    tr = None
    for dc in range(DC):
        b = dc % 2
        w = [t_pmax] + ([bank_free[b]] if bank_free[b] else [])
        tt1 = em.transp(pb[b][:, 0:128], pmax_acc[:, dc * 128:(dc + 1) * 128], IDF,
                        wait=w)
        tr = em.red(pmr[:, dc:dc + 1], pb[b][:, 0:128], ALU.max, wait=[tt1])
        bank_free[b] = tr
    w = [tr] + ([bank_free[2]] if bank_free[2] else [])
    tt2 = em.transp(pb[2][0:8, 0:128], pmr[:, 0:8], IDF, wait=w)
    tpc = em.cp("vector", pmst[0:8, :], pb[2][0:8, 0:128], wait=[tt2])
    bank_free[2] = tpc
    em.dma("scalar", pmaxrow_h.rearrange("o (a b) -> (o a) b", a=8), pmst[0:8, :],
           wait=[tpc], sem="qa0")

    tth = em.ts("vector", thrr, czcarry, float(T // 2), None, ALU.subtract,
                wait=[t_czupd])
    tthd = em.dma("scalar", thrrow_h, thrr, wait=[tth], sem="qa0")
    tthb = em.dma("sync", thr_bc, thrrow_h.broadcast_to([128, 16]), wait=[tthd],
                  sem="qc")

    tlast = None
    for i in range(NTT):
        tlast = em.tt("vector", ind_all[:, i * 16:(i + 1) * 16],
                      cumz_all[:, i * 16:(i + 1) * 16], thr_bc, ALU.is_le,
                      wait=[tthb] if i == 0 else [])
    negb = xf[0][:, 0:NTT * 16]
    wneg = [tlast] + ([xf_free[0]] if xf_free[0] else [])
    em.ts("vector", negb, mask_all, -1.0, 1.0, ALU.mult, ALU.add, wait=wneg)
    em.tt("vector", ind_all, negb, ind_all, ALU.mult)
    tsel = em.tt("vector", ind_all, mask_all, ind_all, ALU.add)
    tse = None
    for i in range(NTT):
        b = 3 + (i % 2)
        w = [tsel] + ([bank_free[b]] if bank_free[b] else [])
        tt3 = em.transp(pb[b][0:16, 0:128], ind_all[:, i * 16:(i + 1) * 16], IDF,
                        wait=w)
        tse = em.cp("vector", selTst[0:16, i * 128:(i + 1) * 128], pb[b][0:16, 0:128],
                    wait=[tt3])
        bank_free[b] = tse
    em.dma("scalar", selT_h, selTst[0:16, :], wait=[tse], sem="qa0")
    if "dbg0" in c["taps"]:
        em.dma("scalar", c["dbg0_h"][:, 0:NTT * 16], mask_all, wait=[tsel], sem="qa0")
        em.dma("scalar", c["dbg0_h"][:, 1024:1024 + NTT * 16], cumz_all, wait=[tsel], sem="qa0")
        em.dma("scalar", c["dbg0_h"][:, 2048:2048 + NTT * 16], ind_all, wait=[tsel], sem="qa0")
        em.dma("scalar", c["dbg0_h"][:, 3072:3072 + 16], thr_bc[:, 0:16], wait=[tsel], sem="qa0")


def _phase1(c, cv):
    nc, em, pb, T, NTT = c["nc"], c["em"], c["pb"], c["T"], c["NTT"]
    LT, IDF, ONESC, ONESR = c["LT"], c["IDF"], c["ONESC"], c["ONESR"]
    eps8 = c["eps8"]
    cs_h = c["cs_h"]
    tclip_h, pnat_h = c["tclip_h"], c["pnat_h"]
    qT_h, kT_h, pmaxrow_h = c["qT_h"], c["kT_h"], c["pmaxrow_h"]
    qraw_h, kraw_h = c["qraw_h"], c["kraw_h"]
    clock_all = c["clock_all"]

    qr = [cv.bf16(D) for _ in range(3)]
    kr = [cv.bf16(D) for _ in range(3)]
    tcl = [cv.f32(D) for _ in range(3)]
    pnl = [cv.f32(D) for _ in range(3)]
    cst = [cv.f32(D), cv.f32(D)]
    tqf = [cv.f32(D), cv.f32(D)]
    tkf = [cv.f32(D), cv.f32(D)]
    qrof = [cv.f32(D), cv.f32(D)]
    krof = [cv.f32(D), cv.f32(D)]
    pexf = [cv.f32(D), cv.f32(D)]
    lnpf = [cv.f32(D), cv.f32(D)]
    uf = [cv.f32(D), cv.f32(D)]
    gjf = [cv.f32(D), cv.f32(D)]
    e1 = [cv.f32(512), cv.f32(512)]
    o1 = [cv.f32(512), cv.f32(512)]
    sqd = cv.f32(D)
    sm = [cv.f32(16), cv.f32(16)]
    stgQ = [cv.bf16(D), cv.bf16(D)]
    stgK = [cv.bf16(D), cv.bf16(D)]
    carryP = cv.f32(D)[0:1, :]
    pmax_bc = cv.f32(D)

    tpm = em.dma("sync", pmax_bc, pmaxrow_h.broadcast_to([128, 1024]), sem="qc")
    tmc = em.v(lambda e: e.memset(carryP, 0.0))

    cs_t = cs_h.rearrange("(n p) d -> n p d", p=128)

    qr_free = [None] * 3
    kr_free = [None] * 3
    tcl_free = [[] for _ in range(3)]
    pnl_free = [None] * 3
    cst_free = [None, None]
    buf_free = {}
    bank_free = [None] * 8
    stgq_free = [None, None]
    stgk_free = [None, None]
    t_carryP = tmc
    for sl in range(2):
        for nm in ("tqf", "tkf", "qrof", "krof", "pexf", "lnpf", "uf", "gjf"):
            buf_free[(nm, sl)] = None

    for i in range(NTT):
        sl = i % 2
        fl = i % 3
        S = sm[sl]
        qs0, qs1, qss = S[:, 0:1], S[:, 1:2], S[:, 2:3]
        rnq = S[:, 3:4]
        ks0, ks1, kss = S[:, 4:5], S[:, 5:6], S[:, 6:7]
        rnk = S[:, 7:8]

        w0 = [tpm] if i == 0 else []
        em.dma("sync", qr[fl], qraw_h[i], sem=f"qs{fl}",
               wait=w0 + ([qr_free[fl]] if qr_free[fl] else []))
        em.dma("sync", kr[fl], kraw_h[i], sem=f"qs{fl}",
               wait=[kr_free[fl]] if kr_free[fl] else [])
        em.dma("sync", tcl[fl], tclip_h[i], wait=tcl_free[fl], sem=f"qs{fl}")
        em.dma("sync", pnl[fl], pnat_h[i], sem=f"qs{fl}",
               wait=[pnl_free[fl]] if pnl_free[fl] else [])
        tls = em.dma("sync", cst[sl], cs_t[i], sem=f"qs{fl}",
                     wait=[cst_free[sl]] if cst_free[sl] else [])
        tl2 = tlc = tlp = tls

        tqs0 = em.act(sqd[:, 0:512], qr[fl][:, 0:512], AF.Square, accum=qs0,
                      wait=[tls])
        tqs1 = em.act(sqd[:, 512:1024], qr[fl][:, 512:1024], AF.Square, accum=qs1)
        em.tt("vector", qss, qs0, qs1, ALU.add, wait=[tqs1])
        t3 = em.ts("vector", qss, qss, 1e-24, None, ALU.max)
        t4 = em.act(qss, qss, AF.Ln, wait=[t3])
        t5 = em.act(rnq, qss, AF.Exp, scale=-0.5, wait=[t4])
        wtq = [t5] + ([buf_free[("tqf", sl)]] if buf_free[("tqf", sl)] else [])
        ttq = em.ts("vector", tqf[sl], qr[fl], rnq, None, ALU.mult, wait=wtq)
        tks0 = em.act(sqd[:, 0:512], kr[fl][:, 0:512], AF.Square, accum=ks0,
                      wait=[tls])
        tks1 = em.act(sqd[:, 512:1024], kr[fl][:, 512:1024], AF.Square, accum=ks1)
        em.tt("vector", kss, ks0, ks1, ALU.add, wait=[tks1])
        t3 = em.ts("vector", kss, kss, 1e-24, None, ALU.max)
        t4 = em.act(kss, kss, AF.Ln, wait=[t3])
        t5 = em.act(rnk, kss, AF.Exp, scale=-0.5, wait=[t4])
        wtk = [t5] + ([buf_free[("tkf", sl)]] if buf_free[("tkf", sl)] else [])
        ttk = em.ts("vector", tkf[sl], kr[fl], rnk, None, ALU.mult, wait=wtk)
        qr_free[fl] = ttq
        kr_free[fl] = ttk

        def rope(src_, dst, dst_free, pre):
            tev = src_[:, 0:1024:2]
            tod = src_[:, 1:1024:2]
            w = [tls, pre] + ([dst_free] if dst_free else [])
            em.tt("vector", e1[sl], tev, cst[sl][:, 0:512], ALU.mult, wait=w)
            em.tt("vector", o1[sl], tod, cst[sl][:, 512:1024], ALU.mult)
            em.tt("vector", dst[:, 0:1024:2], e1[sl], o1[sl], ALU.subtract)
            em.tt("vector", e1[sl], tev, cst[sl][:, 512:1024], ALU.mult)
            em.tt("vector", o1[sl], tod, cst[sl][:, 0:512], ALU.mult)
            return em.tt("vector", dst[:, 1:1024:2], e1[sl], o1[sl], ALU.add)

        tqro = rope(tqf[sl], qrof[sl], buf_free[("qrof", sl)], ttq)
        buf_free[("tqf", sl)] = tqro
        tkro = rope(tkf[sl], krof[sl], buf_free[("krof", sl)], ttk)
        buf_free[("tkf", sl)] = tkro

        wpe = [tlp] + ([buf_free[("pexf", sl)]] if buf_free[("pexf", sl)] else [])
        tpe0 = em.tt("vector", pexf[sl], pnl[fl], pmax_bc, ALU.subtract, wait=wpe)
        pnl_free[fl] = tpe0
        tpe1 = em.act(pexf[sl], pexf[sl], AF.Exp, wait=[tpe0])
        tpe = em.ts("vector", pexf[sl], pexf[sl], clock_all[:, i:i + 1], None,
                    ALU.mult, wait=[tpe1])

        wc = [tpe] + [t for t in (bank_free[6], bank_free[7], t_carryP) if t]
        em.mm(pb[6][:, :], ONESR, carryP[:, 0:512], True, False, wait=wc)
        em.mm(pb[7][:, :], ONESR, carryP[:, 512:1024], True, False)
        em.mm(pb[6][:, :], LT, pexf[sl][:, 0:512], False, True)
        tpc = em.mm(pb[7][:, :], LT, pexf[sl][:, 512:1024], False, True)
        wcol = [tpe] + [t for t in (bank_free[0], bank_free[1]) if t]
        em.mm(pb[0][0:1, 0:512], ONESC, pexf[sl][:, 0:512], True, True, wait=wcol)
        tcol = em.mm(pb[1][0:1, 0:512], ONESC, pexf[sl][:, 512:1024], True, True)
        em.tt("vector", carryP[:, 0:512], carryP[:, 0:512], pb[0][0:1, 0:512],
              ALU.add, wait=[tcol])
        t_carryP = em.tt("vector", carryP[:, 512:1024], carryP[:, 512:1024],
                         pb[1][0:1, 0:512], ALU.add)
        bank_free[0] = t_carryP
        bank_free[1] = t_carryP

        wlnp = [tpc] + ([buf_free[("lnpf", sl)]] if buf_free[("lnpf", sl)] else [])
        em.act(lnpf[sl][:, 0:512], pb[6][:, :], AF.Ln, bias=eps8, wait=wlnp)
        tlnp = em.act(lnpf[sl][:, 512:1024], pb[7][:, :], AF.Ln, bias=eps8)
        bank_free[6] = tlnp
        bank_free[7] = tlnp
        wu = [tlnp, tlc] + ([buf_free[("uf", sl)]] if buf_free[("uf", sl)] else [])
        tu0 = em.stt("vector", uf[sl], tcl[fl], -1.0, lnpf[sl], ALU.mult,
                     ALU.subtract, wait=wu)
        buf_free[("lnpf", sl)] = tu0
        tu1 = em.act(uf[sl], uf[sl], AF.Exp, wait=[tu0])
        tqp = em.tt("vector", qrof[sl], qrof[sl], uf[sl], ALU.mult,
                    wait=[tu1, tqro])
        buf_free[("uf", sl)] = tqp

        wgj = [tlc] + ([buf_free[("gjf", sl)]] if buf_free[("gjf", sl)] else [])
        tg0 = em.act(gjf[sl], tcl[fl], AF.Exp, scale=-1.0, wait=wgj)
        tcl_free[fl] = [tu0, tg0]
        tg1 = em.act(gjf[sl], gjf[sl], AF.Ln, bias=eps8, wait=[tg0])
        tg2 = em.act(gjf[sl], gjf[sl], AF.Exp, scale=-1.0, wait=[tg1])
        tk1 = em.tt("vector", krof[sl], krof[sl], pexf[sl], ALU.mult,
                    wait=[tkro, tpe])
        buf_free[("pexf", sl)] = tk1
        tkp = em.tt("vector", krof[sl], krof[sl], gjf[sl], ALU.mult, wait=[tg2])
        buf_free[("gjf", sl)] = tkp

        tAq = tAk = None
        ttq_last = ttk_last = None
        for dc in range(DC):
            bq = 2 + dc % 2
            wq = [tqp] + ([bank_free[bq]] if bank_free[bq] else []) \
                + ([stgq_free[sl]] if (dc == 0 and stgq_free[sl]) else [])
            t6 = em.transp(pb[bq][:, 0:128], qrof[sl][:, dc * 128:(dc + 1) * 128],
                           IDF, wait=wq)
            tAq = em.cp("vector", stgQ[sl][:, dc * 128:(dc + 1) * 128],
                        pb[bq][:, 0:128], wait=[t6])
            bank_free[bq] = tAq
            bk = 4 + dc % 2
            wk = [tkp] + ([bank_free[bk]] if bank_free[bk] else []) \
                + ([stgk_free[sl]] if (dc == 0 and stgk_free[sl]) else [])
            t7 = em.transp(pb[bk][:, 0:128], krof[sl][:, dc * 128:(dc + 1) * 128],
                           IDF, wait=wk)
            tAk = em.act(stgK[sl][:, dc * 128:(dc + 1) * 128], pb[bk][:, 0:128],
                         AF.Identity, wait=[t7])
            bank_free[bk] = tAk
            ttq_last, ttk_last = t6, t7
        buf_free[("qrof", sl)] = ttq_last
        buf_free[("krof", sl)] = ttk_last
        cst_free[sl] = tkro
        em.dma("scalar",
               qT_h[i // 2].rearrange("p (c u n) -> p c u n", c=DC, u=2)[:, :, i % 2, :],
               stgQ[sl].rearrange("p (c n) -> p c n", c=DC),
               wait=[tAq], sem=f"qa{sl}")
        tdk = em.dma("scalar", kT_h[i], stgK[sl], wait=[tAk], sem=f"qa{sl}")
        stgq_free[sl] = tdk
        stgk_free[sl] = tdk


def _phase2(c, cv):
    nc, em, pb, T, NTT = c["nc"], c["em"], c["pb"], c["T"], c["NTT"]
    IDF, ONESR = c["IDF"], c["ONESR"]
    qT_h, kT_h, vnat_h, xnT_h, x2T_h = (c["qT_h"], c["kT_h"], c["vnat_h"],
                                        c["xnT_h"], c["x2T_h"])
    wout_h, cbf = c["wout_h"], c["cbf"]
    outb_c = c["outb_c"]
    NQB = T // 256

    v_sb = cv.bf16(NTT * D)
    wout_sb = cv.bf16(DC * D)
    qblk = [cv.bf16(2 * D), cv.bf16(2 * D)]
    xnt2 = [cv.bf16(2 * D), cv.bf16(2 * D)]
    kbuf = [cv.bf16(D) for _ in range(4)]
    wsl = [cv.bf16(256), cv.bf16(256), cv.bf16(256)]
    attb = cv.bf16(2 * D)
    x2st = [cv.bf16(2 * D), cv.bf16(2 * D)]
    rbc = cv.f32(256)
    zr = cv.f32(256)[0:1, :]

    MASK0 = cbf[:, 0:256]
    MASK1 = cbf[:, 256:512]
    ONESC_B = cbf[:, 512:513]

    tw = em.dma("gpsimd", wout_sb, wout_h)
    tv = None
    for kt in range(NTT):
        tv = em.dma("sync", v_sb[:, kt * D:(kt + 1) * D], vnat_h[kt], sem="qc")
    w0 = [tw, tv]

    qblk_free = [None, None]
    kbuf_free = [None] * 4
    x2_free = [None, None]
    attb_free = None
    bank_free = [None] * 8
    SBANKS = (0, 1, 7)
    sbank_free = [None, None, None]
    zr_free = None
    rbc_free = None

    for qb in range(NQB):
        sl = qb % 2
        nkt = 2 * qb + 2
        wq = (w0 if qb == 0 else []) + ([qblk_free[sl]] if qblk_free[sl] else [])
        tq = em.dma("sync", qblk[sl], qT_h[qb], sem=f"qs{sl}", wait=wq)
        tx = em.dma("sync", xnt2[sl], xnT_h[qb], sem=f"qs{sl}",
                    wait=[x2_free[sl]] if x2_free[sl] else [])
        tq = tx

        t_exp = {}
        t_av = {}
        for kt in range(nkt):
            kb = kt % 4
            wk = [t_av[kt - 4]] if kt - 4 in t_av else []
            if qb > 0 and kt < 4 and kbuf_free[kb]:
                wk = wk + [kbuf_free[kb]]
            tk = em.dma("sync", kbuf[kb], kT_h[kt], sem=f"qk{kb}", wait=wk)
            sbk = pb[SBANKS[kt % 3]]
            wS = [tk, tq]
            if kt - 3 in t_exp:
                wS.append(t_exp[kt - 3])
            elif kt < 3 and sbank_free[kt % 3]:
                wS.append(sbank_free[kt % 3])
            tS = None
            for dc in range(DC):
                tS = em.mm(sbk[:, 0:256], kbuf[kb][:, dc * 128:(dc + 1) * 128],
                           qblk[sl][:, dc * 256:(dc + 1) * 256],
                           (dc == 0), (dc == DC - 1), wait=wS if dc == 0 else [])
            wE = [tS] + ([t_av[kt - 3]] if kt - 3 in t_av else [])
            if kt >= nkt - 2:
                tcl_ = em.ts("vector", sbk[:, 0:256], sbk[:, 0:256], 1.0, None,
                             ALU.min, wait=wE)
                wE = [tcl_]
            tE = em.act(wsl[kt % 3], sbk[:, 0:256], AF.Exp, scale=1.0 / 32.0,
                        wait=wE)
            if kt >= nkt - 2:
                msk = MASK0 if kt == nkt - 2 else MASK1
                tE = em.tt("vector", wsl[kt % 3], wsl[kt % 3], msk, ALU.mult,
                           wait=[tE])
            t_exp[kt] = tE
            wZ = [tE]
            if qb > 0 and kt == 0:
                wZ += [t for t in (bank_free[6], bank_free[2], bank_free[3],
                                   bank_free[4], bank_free[5]) if t]
            em.mm(pb[6][0:1, 0:256], ONESC_B, wsl[kt % 3],
                  (kt == 0), (kt == nkt - 1), wait=wZ)
            tAV = None
            for dc in range(DC):
                tAV = em.mm(pb[2 + dc // 2][:, (dc % 2) * 256:(dc % 2) * 256 + 256],
                            v_sb[:, kt * D + dc * 128:kt * D + (dc + 1) * 128],
                            wsl[kt % 3],
                            (kt == 0 and dc % 2 == 0),
                            (kt == nkt - 1 and dc % 2 == 1))
            t_av[kt] = tAV
        for j_ in range(min(4, nkt)):
            kbuf_free[(nkt - 1 - j_) % 4] = t_av[nkt - 1 - j_]
        for j_ in range(min(3, nkt)):
            sbank_free[(nkt - 1 - j_) % 3] = t_exp[nkt - 1 - j_]
        qblk_free[sl] = tS

        wz1 = [t_av[nkt - 1]] + ([zr_free] if zr_free else [])
        tz0 = em.cp("vector", zr, pb[6][0:1, 0:256], wait=wz1)
        tz1 = em.v(lambda e, o=zr: e.reciprocal(o, o), wait=[tz0])
        bank_free[6] = tz1
        kt_last7 = max(k_ for k_ in range(nkt) if k_ % 3 == 2) if nkt >= 3 else None
        wR = [tz1]
        if kt_last7 is not None:
            wR.append(t_exp[kt_last7])
        if bank_free[7]:
            wR.append(bank_free[7])
        tR = em.mm(pb[7][:, 0:256], ONESR, zr, True, True, wait=wR)
        wrb = [tR] + ([rbc_free] if rbc_free else [])
        trb = em.cp("vector", rbc, pb[7][:, 0:256], wait=wrb)
        bank_free[7] = trb
        sbank_free[2] = trb
        tatt = None
        wA0 = [trb] + ([attb_free] if attb_free else [])
        for dc in range(DC):
            tatt = em.tt("vector", attb[:, dc * 256:(dc + 1) * 256],
                         pb[2 + dc // 2][:, (dc % 2) * 256:(dc % 2) * 256 + 256],
                         rbc, ALU.mult, wait=wA0 if dc == 0 else [])
        zr_free = tatt

        tX = {}
        for mc in range(DC):
            wX = [tatt] if mc == 0 else []
            for ki in range(DC):
                tXm = em.mm(pb[2 + mc // 2][:, (mc % 2) * 256:(mc % 2) * 256 + 256],
                            wout_sb[:, ki * D + mc * 128:ki * D + (mc + 1) * 128],
                            attb[:, ki * 256:(ki + 1) * 256],
                            (ki == 0 and mc % 2 == 0),
                            (ki == DC - 1 and mc % 2 == 1),
                            wait=wX if ki == 0 else [])
            tX[mc] = tXm
        attb_free = tX[DC - 1]
        tE2 = None
        for mc in range(DC):
            wE2 = [tX[mc | 1]] + ([x2_free[sl]] if (mc == 0 and x2_free[sl]) else [])
            tE2 = em.stt("vector", x2st[sl][:, mc * 256:(mc + 1) * 256],
                         pb[2 + mc // 2][:, (mc % 2) * 256:(mc % 2) * 256 + 256],
                         outb_c[mc], xnt2[sl][:, mc * 256:(mc + 1) * 256],
                         ALU.add, ALU.add, wait=wE2)
            if mc // 2 == 0:
                bank_free[2] = tE2
            elif mc // 2 == 1:
                bank_free[3] = tE2
            elif mc // 2 == 2:
                bank_free[4] = tE2
            else:
                bank_free[5] = tE2
        td = em.dma("scalar", x2T_h[qb], x2st[sl], wait=[tE2], sem=f"qa{sl}")
        x2_free[sl] = td


def _phase3(c, cv):
    nc, em, pb, T, NTT = c["nc"], c["em"], c["pb"], c["T"], c["NTT"]
    ONESC, ONESR = c["ONESC"], c["ONESR"]
    eps5 = c["eps5"]
    xnT_h, selT_h, gT_h = c["xnT_h"], c["selT_h"], c["gT_h"]
    wmoe1_h, wmoe2_h, b2cat_h = c["wmoe1_h"], c["wmoe2_h"], c["b2cat_h"]
    gam_c, bet_c, b1cat_c = c["gam_c"], c["bet_c"], c["b1cat_c"]
    NQB = T // 256

    w1_sb = cv.bf16(DC * DFF)
    w2_sb = cv.bf16(32 * D)
    b2_sb = cv.bf16(D)
    xnb = [cv.bf16(2 * D), cv.bf16(2 * D)]
    selb = [cv.bf16(16 * 256), cv.bf16(16 * 256)]
    selr = [cv.bf16(256), cv.bf16(256)]
    Hm = cv.bf16(32 * 256)
    zf = cv.f32(2 * D)
    sqz = cv.f32(256)
    tmp = cv.f32(256)
    gst = [cv.bf16(2 * D), cv.bf16(2 * D)]
    mur = cv.f32(256)[0:1, :]
    rsr = cv.f32(256)[0:1, :]

    tw1 = em.dma("gpsimd", w1_sb, wmoe1_h)
    tw2 = em.dma("gpsimd", w2_sb, wmoe2_h)
    tb2 = em.dma("gpsimd", b2_sb[0:16, 0:D], b2cat_h)
    w0 = [tw1, tw2, tb2]

    xnb_free = [None, None]
    selb_free = [None, None]
    g_free = [None, None]
    Hm_free = None
    zf_free = None
    bank_free = [None] * 8
    row_free = None

    for j in range(NQB):
        sl = j % 2
        wx = (w0 if j == 0 else []) + ([xnb_free[sl]] if xnb_free[sl] else [])
        tlx = em.dma("sync", xnb[sl], xnT_h[j], sem=f"qs{sl}", wait=wx)
        wsel = [selb_free[sl]] if selb_free[sl] else []
        for e in range(E):
            em.dma("sync", selb[sl][:, e * 256:(e + 1) * 256],
                   selT_h[e:e + 1, j * 256:(j + 1) * 256].broadcast_to([128, 256]),
                   sem=f"qk{sl}", wait=wsel if e == 0 else [])
        tsel = em.dma("sync", selr[sl][0:16, :], selT_h[:, j * 256:(j + 1) * 256],
                      sem=f"qk{sl}")

        tgel = {}
        tmask = {}
        for mc in range(32):
            b = mc % 4
            wH = ([tlx] if mc == 0 else []) \
                + ([tmask[mc - 4]] if mc - 4 in tmask else []) \
                + ([bank_free[b]] if (mc < 4 and bank_free[b]) else [])
            tH = None
            for ki in range(DC):
                tH = em.mm(pb[b][:, 0:256],
                           w1_sb[:, ki * DFF + mc * 128:ki * DFF + (mc + 1) * 128],
                           xnb[sl][:, ki * 256:(ki + 1) * 256],
                           (ki == 0), (ki == DC - 1), wait=wH if ki == 0 else [])
            wG = [tH] + ([Hm_free] if (mc == 0 and Hm_free) else [])
            tg = em.act(Hm[:, mc * 256:(mc + 1) * 256], pb[b][:, 0:256], AF.Gelu,
                        bias=b1cat_c[mc], wait=wG)
            tgel[mc] = tg
            wM = [tg] + ([tsel] if mc == 0 else [])
            tmask[mc] = em.tt("vector", Hm[:, mc * 256:(mc + 1) * 256],
                              Hm[:, mc * 256:(mc + 1) * 256],
                              selb[sl][:, (mc // 2) * 256:(mc // 2 + 1) * 256],
                              ALU.mult, wait=wM)
        selb_free[sl] = tmask[31]
        t_lastmask = tmask[31]
        for b_ in range(4):
            bank_free[b_] = tgel[28 + b_]

        tz = None
        tz_prev = {}
        for mc in range(DC):
            b = 4 + mc % 2
            wO = [t_lastmask] if mc == 0 else []
            if mc < 2:
                if bank_free[b]:
                    wO.append(bank_free[b])
            else:
                wO.append(tz_prev[b])
            for ki in range(32):
                em.mm(pb[b][:, 0:256],
                      w2_sb[:, ki * D + mc * 128:ki * D + (mc + 1) * 128],
                      Hm[:, ki * 256:(ki + 1) * 256],
                      (ki == 0), False, wait=wO if ki == 0 else [])
            tO = em.mm(pb[b][:, 0:256], b2_sb[0:16, mc * 128:(mc + 1) * 128],
                       selr[sl][0:16, :], False, True)
            wz = [tO] + ([zf_free] if (mc == 0 and zf_free) else [])
            tz = em.tt("vector", zf[:, mc * 256:(mc + 1) * 256], pb[b][:, 0:256],
                       xnb[sl][:, mc * 256:(mc + 1) * 256], ALU.add, wait=wz)
            tz_prev[b] = tz
            bank_free[b] = tz
        Hm_free = tz
        xnb_free[sl] = tz

        wst = [tz] + ([bank_free[6]] if bank_free[6] else [])
        tmu = None
        for mc in range(DC):
            tmu = em.mm(pb[6][0:1, 0:256], ONESC, zf[:, mc * 256:(mc + 1) * 256],
                        (mc == 0), (mc == DC - 1), wait=wst if mc == 0 else [])
        wmu = [tmu] + ([row_free] if row_free else [])
        tmur = em.ts("vector", mur, pb[6][0:1, 0:256], 1.0 / D, None, ALU.mult,
                     wait=wmu)
        tsq_ = None
        for mc in range(DC):
            wsq = [tmur] if mc == 0 else [tsq_]
            ts0 = em.act(sqz, zf[:, mc * 256:(mc + 1) * 256], AF.Square, wait=wsq)
            tsq_ = em.mm(pb[6][0:1, 0:256], ONESC, sqz, (mc == 0), (mc == DC - 1),
                         wait=[ts0])
        tvm = em.ts("vector", rsr, pb[6][0:1, 0:256], 1.0 / D, None, ALU.mult,
                    wait=[tsq_])
        bank_free[6] = tvm
        tm2 = em.tt("vector", tmp[0:1, 0:256], mur, mur, ALU.mult, wait=[tmur])
        tvr = em.tt("vector", rsr, rsr, tmp[0:1, 0:256], ALU.subtract, wait=[tvm, tm2])
        tl_ = em.act(rsr, rsr, AF.Ln, bias=eps5[0:1, :], wait=[tvr])
        trs = em.act(rsr, rsr, AF.Exp, scale=-0.5, wait=[tl_])
        wb1 = [tmur, tvr] + ([bank_free[7]] if bank_free[7] else [])
        tbm = em.mm(pb[7][:, 0:256], ONESR, mur, True, True, wait=wb1)
        tbr = em.mm(pb[6][:, 0:256], ONESR, rsr, True, True, wait=[trs])
        row_free = tbr
        tgst = None
        for mc in range(DC):
            wn = [tbm, tbr]
            if mc == 0 and g_free[sl]:
                wn.append(g_free[sl])
            t1 = em.tt("vector", zf[:, mc * 256:(mc + 1) * 256],
                       zf[:, mc * 256:(mc + 1) * 256], pb[7][:, 0:256],
                       ALU.subtract, wait=wn)
            t2 = em.tt("vector", zf[:, mc * 256:(mc + 1) * 256],
                       zf[:, mc * 256:(mc + 1) * 256], pb[6][:, 0:256],
                       ALU.mult, wait=[t1])
            t3 = em.ts("vector", zf[:, mc * 256:(mc + 1) * 256],
                       zf[:, mc * 256:(mc + 1) * 256], gam_c[mc], bet_c[mc],
                       ALU.mult, ALU.add, wait=[t2])
            tgst = em.act(gst[sl][:, mc * 256:(mc + 1) * 256],
                          zf[:, mc * 256:(mc + 1) * 256], AF.Silu, wait=[t3])
        zf_free = tgst
        bank_free[7] = tgst
        bank_free[6] = tgst
        tgd = em.dma("scalar", gT_h[j], gst[sl], wait=[tgst], sem=f"qa{sl}")
        g_free[sl] = tgd


def _phase4(c, cv):
    nc, em, pb, T, NTT = c["nc"], c["em"], c["pb"], c["T"], c["NTT"]
    ONESC, ONESR, IDF = c["ONESC"], c["ONESR"], c["IDF"]
    eps5 = c["eps5"]
    x_h, out_h = c["x_h"], c["out_h"]
    x2T_h, gT_h = c["x2T_h"], c["gT_h"]
    wmlp1_h, wmlp2_h = c["wmlp1_h"], c["wmlp2_h"]
    ln2g_c, ln2b_c, mlpb1_c, mlpb2_c = (c["ln2g_c"], c["ln2b_c"],
                                        c["mlpb1_c"], c["mlpb2_c"])
    ONESC_B = c["cbf"][:, 512:513]
    NQB = T // 256

    w1_sb = cv.bf16(DC * DFF)
    w2_sb = cv.bf16(32 * D)
    x2b = [cv.bf16(2 * D), cv.bf16(2 * D)]
    gb2 = [cv.bf16(2 * D), cv.bf16(2 * D)]
    xb = cv.f32(2 * D)
    Hm = cv.bf16(32 * 256)
    hst = cv.bf16(2 * D)
    sf = cv.f32(2 * D)
    sqz = cv.f32(256)
    tmp = cv.f32(256)
    outt = [cv.f32(D), cv.f32(D)]
    mur = cv.f32(256)[0:1, :]
    rsr = cv.f32(256)[0:1, :]

    tw1 = em.dma("gpsimd", w1_sb, wmlp1_h)
    tw2 = em.dma("gpsimd", w2_sb, wmlp2_h)
    w0 = [tw1, tw2]

    x_t = x_h.rearrange("(n p) d -> n p d", p=128)
    out_t = out_h.rearrange("(n p) d -> n p d", p=128)

    x2_free = [None, None]
    g_free = [None, None]
    xb_free = None
    Hm_free = None
    hst_free = None
    sf_free = None
    outt_free = [None, None]
    bank_free = [None] * 8
    row_free = None

    for j in range(NQB):
        sl = j % 2
        wx = (w0 if j == 0 else []) + ([x2_free[sl]] if x2_free[sl] else [])
        em.dma("sync", x2b[sl], x2T_h[j], sem=f"qs{sl}", wait=wx)
        em.dma("sync", gb2[sl], gT_h[j], sem=f"qs{sl}",
               wait=[g_free[sl]] if g_free[sl] else [])
        em.dma("sync", xb[:, 0:D], x_t[2 * j], sem=f"qs{sl}",
               wait=[xb_free] if xb_free else [])
        tlx = em.dma("sync", xb[:, D:2 * D], x_t[2 * j + 1], sem=f"qs{sl}")

        wst = [tlx] + ([bank_free[6]] if bank_free[6] else [])
        tmu = None
        for mc in range(DC):
            tmu = em.mm(pb[6][0:1, 0:256], ONESC_B,
                        x2b[sl][:, mc * 256:(mc + 1) * 256],
                        (mc == 0), (mc == DC - 1), wait=wst if mc == 0 else [])
        wmu = [tmu] + ([row_free] if row_free else [])
        tmur = em.ts("vector", mur, pb[6][0:1, 0:256], 1.0 / D, None, ALU.mult,
                     wait=wmu)
        tsq_ = None
        for mc in range(DC):
            wsq = [tmur] if mc == 0 else [tsq_]
            ts0 = em.act(sqz, x2b[sl][:, mc * 256:(mc + 1) * 256], AF.Square,
                         wait=wsq)
            tsq_ = em.mm(pb[6][0:1, 0:256], ONESC, sqz, (mc == 0), (mc == DC - 1),
                         wait=[ts0])
        tvm = em.ts("vector", rsr, pb[6][0:1, 0:256], 1.0 / D, None, ALU.mult,
                    wait=[tsq_])
        bank_free[6] = tvm
        tm2 = em.tt("vector", tmp[0:1, 0:256], mur, mur, ALU.mult, wait=[tmur])
        tvr = em.tt("vector", rsr, rsr, tmp[0:1, 0:256], ALU.subtract,
                    wait=[tvm, tm2])
        tl_ = em.act(rsr, rsr, AF.Ln, bias=eps5[0:1, :], wait=[tvr])
        trs = em.act(rsr, rsr, AF.Exp, scale=-0.5, wait=[tl_])
        wb1 = [tmur, tvr] + ([bank_free[7]] if bank_free[7] else [])
        tbm = em.mm(pb[7][:, 0:256], ONESR, mur, True, True, wait=wb1)
        tbr = em.mm(pb[6][:, 0:256], ONESR, rsr, True, True, wait=[trs])
        row_free = tbr
        th_ = None
        for mc in range(DC):
            wn = [tbm, tbr]
            if mc == 0 and hst_free:
                wn.append(hst_free)
            t1 = em.tt("vector", sf[:, mc * 256:(mc + 1) * 256],
                       x2b[sl][:, mc * 256:(mc + 1) * 256], pb[7][:, 0:256],
                       ALU.subtract, wait=wn + ([sf_free] if (mc == 0 and sf_free) else []))
            t2 = em.tt("vector", sf[:, mc * 256:(mc + 1) * 256],
                       sf[:, mc * 256:(mc + 1) * 256], pb[6][:, 0:256],
                       ALU.mult, wait=[t1])
            th_ = em.ts("vector", hst[:, mc * 256:(mc + 1) * 256],
                        sf[:, mc * 256:(mc + 1) * 256], ln2g_c[mc], ln2b_c[mc],
                        ALU.mult, ALU.add, wait=[t2])
        bank_free[7] = th_
        bank_free[6] = th_

        tgel = {}
        for mc in range(32):
            b = mc % 4
            wH = ([th_] if mc == 0 else []) \
                + ([tgel[mc - 4]] if mc - 4 in tgel else []) \
                + ([bank_free[b]] if (mc < 4 and bank_free[b]) else [])
            tH = None
            for ki in range(DC):
                tH = em.mm(pb[b][:, 0:256],
                           w1_sb[:, ki * DFF + mc * 128:ki * DFF + (mc + 1) * 128],
                           hst[:, ki * 256:(ki + 1) * 256],
                           (ki == 0), (ki == DC - 1), wait=wH if ki == 0 else [])
            wG = [tH] + ([Hm_free] if (mc == 0 and Hm_free) else [])
            tgel[mc] = em.act(Hm[:, mc * 256:(mc + 1) * 256], pb[b][:, 0:256],
                              AF.Gelu, bias=mlpb1_c[mc], wait=wG)
        hst_free = tH
        t_lastH = tgel[31]
        bank_free[2] = tgel[30]
        bank_free[3] = tgel[31]

        tsg = None
        ts_prev = {}
        for mc in range(DC):
            b = 4 + mc % 2
            wO = [t_lastH] if mc == 0 else []
            if mc < 2:
                if bank_free[b]:
                    wO.append(bank_free[b])
            else:
                wO.append(ts_prev[b])
            tO = None
            for ki in range(32):
                tO = em.mm(pb[b][:, 0:256],
                           w2_sb[:, ki * D + mc * 128:ki * D + (mc + 1) * 128],
                           Hm[:, ki * 256:(ki + 1) * 256],
                           (ki == 0), (ki == 31), wait=wO if ki == 0 else [])
            ws_ = [tO]
            ts1 = em.stt("vector", sf[:, mc * 256:(mc + 1) * 256], pb[b][:, 0:256],
                         mlpb2_c[mc], x2b[sl][:, mc * 256:(mc + 1) * 256],
                         ALU.add, ALU.add, wait=ws_)
            ts_prev[b] = ts1
            bank_free[b] = ts1
            tsg = em.tt("vector", sf[:, mc * 256:(mc + 1) * 256],
                        sf[:, mc * 256:(mc + 1) * 256],
                        gb2[sl][:, mc * 256:(mc + 1) * 256], ALU.mult, wait=[ts1])
        sf_free_local = tsg
        x2_free[sl] = tsg
        g_free[sl] = tsg
        Hm_free = tO

        tev = None
        for par in range(2):
            for dc in range(DC):
                b = dc % 2
                wT = [tsg] + ([bank_free[b]] if bank_free[b] else []) \
                    + ([outt_free[par]] if (dc == 0 and outt_free[par]) else [])
                tt1 = em.transp(pb[b][:, 0:128],
                                sf[:, dc * 256 + par * 128:dc * 256 + par * 128 + 128],
                                IDF, wait=wT)
                tev = em.tt("vector", outt[par][:, dc * 128:(dc + 1) * 128],
                            pb[b][:, 0:128], xb[:, par * D + dc * 128:par * D + (dc + 1) * 128],
                            ALU.add, wait=[tt1])
                bank_free[b] = tev
            tod = em.dma("scalar", out_t[2 * j + par], outt[par], wait=[tev],
                         sem=f"qa{par}")
            outt_free[par] = tod
        xb_free = tev
        sf_free = tev



_NC_CACHE = {}


def _get_nc(T, taps=()):
    key = (T, tuple(taps), _PHASES)
    if key not in _NC_CACHE:
        _NC_CACHE[key] = build_nc(T, taps)
    return _NC_CACHE[key]


def kernel(**inputs):
    x = np.asarray(inputs["L0_attn_x"], dtype=np.float32)
    B, T, _ = x.shape
    shared = _host_inputs(inputs, T)
    nc = _get_nc(T)
    in_maps = []
    for b in range(B):
        m = dict(shared)
        m["x"] = np.ascontiguousarray(x[b])
        in_maps.append(m)
    res = run_bass_kernel_spmd(nc, in_maps, core_ids=list(range(B)))
    out = np.stack([np.asarray(r["out"], dtype=np.float32) for r in res.results])
    return out




# version 15
# speedup vs baseline: 1.1018x; 1.1018x over previous
import math
import numpy as np
import ml_dtypes
import concourse.bass as bass
import concourse.mybir as mybir
from concourse.bass_utils import run_bass_kernel_spmd

F32 = mybir.dt.float32
BF16 = mybir.dt.bfloat16
AF = mybir.ActivationFunctionType
ALU = mybir.AluOpType
AX = mybir.AxisListType

D = 1024
DC = 8
E = 16
FP = 256
DFF = 4096


def _host_consts(T):
    half = D // 2
    freqs = np.exp(-np.arange(half, dtype=np.float64) * (math.log(10000.0) / half))
    times = np.arange(T, dtype=np.float32).astype(np.float64)
    th = (times[:, None].astype(np.float32) * freqs[None, :].astype(np.float32)).astype(np.float32)
    cs = np.empty((T, D), dtype=np.float32)
    cs[:, :half] = np.cos(th.astype(np.float64)).astype(np.float32)
    cs[:, half:] = np.sin(th.astype(np.float64)).astype(np.float32)

    LT = np.triu(np.ones((128, 128), dtype=np.float32))
    idf = np.eye(128, dtype=np.float32)
    cf32 = np.concatenate([LT, idf, np.ones((128, 1), np.float32)], axis=1)

    kk = np.arange(128)[:, None]
    qq = np.arange(256)[None, :]
    m0 = (kk <= qq).astype(ml_dtypes.bfloat16)
    m1 = (kk <= qq - 128).astype(ml_dtypes.bfloat16)
    cbf = np.concatenate([m0, m1, np.ones((128, 1), ml_dtypes.bfloat16)], axis=1)
    return cs, cf32, cbf


def _pack_col(vec):
    C = vec.shape[0] // 128
    return np.ascontiguousarray(vec.reshape(C, 128).T)


def _host_inputs(inputs, T):
    g = lambda k: np.asarray(inputs[k], dtype=np.float32)

    def flatw(w):
        Kd, Nd = w.shape
        return np.ascontiguousarray(
            w.reshape(Kd // 128, 128, Nd).transpose(1, 0, 2).reshape(128, -1))

    wqkv = np.concatenate([flatw(g("L0_attn_Wq")), flatw(g("L0_attn_Wk")),
                           flatw(g("L0_attn_Wv"))], axis=1)
    wgp = np.concatenate([flatw(g("L0_attn_gate_W")), flatw(g("L0_attn_p_W"))],
                         axis=1)
    wmisc = np.ascontiguousarray(
        np.concatenate([g("L0_attn_clock_W").reshape(DC, 128, 1),
                        g("L0_ffn_wg").reshape(DC, 128, E)], axis=2)
        .transpose(1, 0, 2).reshape(128, DC * 17))
    wout = flatw(g("L0_attn_out_W"))
    wmlp1 = flatw(g("L0_attn_mlp_W1"))
    wmlp2 = flatw(g("L0_attn_mlp_W2"))
    w1cat = np.ascontiguousarray(g("L0_ffn_w1").transpose(1, 0, 2).reshape(D, E * FP))
    wmoe1 = flatw(w1cat)
    wmoe2 = flatw(np.ascontiguousarray(g("L0_ffn_w2").reshape(E * FP, D)))
    b2cat = g("L0_ffn_b2")

    ppack = np.concatenate([
        _pack_col(g("L0_attn_ln1_g")), _pack_col(g("L0_attn_ln1_b")),
        _pack_col(g("L0_attn_ln2_g")), _pack_col(g("L0_attn_ln2_b")),
        _pack_col(g("L0_ffn_gamma")), _pack_col(g("L0_ffn_beta")),
        _pack_col(g("L0_attn_out_b")), _pack_col(g("L0_attn_mlp_b2")),
        _pack_col(g("L0_attn_mlp_b1")), _pack_col(g("L0_ffn_b1").reshape(-1)),
    ], axis=1)

    prow = np.zeros((1, 2072), dtype=np.float32)
    prow[0, 0:1024] = g("L0_attn_gate_b")
    prow[0, 1024:2048] = g("L0_attn_p_b")
    prow[0, 2048:2064] = g("L0_ffn_bg")
    prow[0, 2064] = g("L0_attn_clock_b")[0]
    prow[0, 2065] = 1e-5
    prow[0, 2066] = 1e-8
    prow[0, 2067] = 1.0
    prow[0, 2068] = float(T // 2)

    cs, cf32, cbf = _host_consts(T)
    return dict(wqkv=wqkv, wgp=wgp, wmisc=wmisc, wout=wout, wmlp1=wmlp1,
                wmlp2=wmlp2, wmoe1=wmoe1, wmoe2=wmoe2, b2cat=b2cat,
                ppack=ppack, prow=prow, cs_tab=cs, cf32=cf32, cbf16=cbf)



ENGINES = ("tensor", "vector", "scalar", "gpsimd", "sync")


class Em:

    def __init__(self, nc):
        self.nc = nc
        self.ops = {e: [] for e in ENGINES}
        self.sem = {}
        self.cnt = {}
        self.hwm = {e: {} for e in ENGINES}
        self._es = None

    def add_sem(self, name, handle):
        self.sem[name] = handle
        self.cnt[name] = 0

    def op(self, eng, fn, wait=(), sig=None, dma=False, osz=None):
        tok = None
        if sig is not None:
            self.cnt[sig] += 16 if dma else 1
            tok = (sig, self.cnt[sig])
        for (s, v) in wait:
            assert v <= self.cnt[s], f"wait on future token {s}={v} (now {self.cnt[s]})"
        self.ops[eng].append((tuple(wait), fn,
                              (sig, 16 if dma else 1) if sig else None, osz))
        return tok

    @staticmethod
    def _osz(out):
        try:
            sh = out.shape
        except Exception:
            return None
        n = 1
        for d in list(sh)[1:]:
            n *= int(d)
        return n

    def t(self, fn, wait=(), sig="te"):
        return self.op("tensor", fn, wait, sig)

    def v(self, fn, wait=(), sig="ve"):
        return self.op("vector", fn, wait, sig)

    def s(self, fn, wait=(), sig="sc"):
        return self.op("scalar", fn, wait, sig)

    def g(self, fn, wait=(), sig="gp"):
        return self.op("gpsimd", fn, wait, sig)

    def dma(self, queue, out, in_, wait=(), sem=None):
        semname = sem or {"sync": "qs0", "scalar": "qa0", "gpsimd": "qg"}[queue]
        return self.op(queue, lambda e: e.dma_start(out=out, in_=in_),
                       wait, semname, dma=True)

    def barrier(self):
        snap = {s: self.cnt[s] for s in self.cnt}
        w = [(s, v) for s, v in snap.items() if v > 0]
        for e in ENGINES:
            self.op(e, lambda _e: None, wait=w)
        return snap

    def mm(self, out, lhsT, rhs, start, stop, wait=(), sig="te"):
        return self.op("tensor", lambda e, o=out, l=lhsT, r=rhs, a=start, b=stop:
                       e.matmul(o, l, r, start=a, stop=b), wait, sig)

    def transp(self, out, in_, ident, wait=(), sig="te"):
        return self.op("tensor", lambda e, o=out, i=in_, d=ident: e.transpose(o, i, d),
                       wait, sig)

    def tt(self, eng, out, a, b, op, wait=(), sig=None):
        sig = sig or {"vector": "ve", "gpsimd": "gp"}[eng]
        return self.op(eng, lambda e, o=out, x=a, y=b, p=op: e.tensor_tensor(o, x, y, p),
                       wait, sig, osz=self._osz(out))

    def ts(self, eng, out, in0, s1, s2, op0, op1=None, accum=None, wait=(), sig=None):
        sig = sig or {"vector": "ve", "gpsimd": "gp"}[eng]
        if op1 is None:
            return self.op(eng, lambda e, o=out, i=in0, a=s1, p=op0:
                           e.tensor_single_scalar(o, i, a, p), wait, sig,
                           osz=self._osz(out))
        return self.op(eng, lambda e, o=out, i=in0, a=s1, b=s2, p=op0, q=op1, ac=accum:
                       e.tensor_scalar(o, i, a, b, p, q, accum_out=ac), wait, sig,
                       osz=self._osz(out))

    def stt(self, eng, out, in0, scalar, in1, op0, op1, wait=(), sig=None):
        sig = sig or {"vector": "ve", "gpsimd": "gp"}[eng]
        return self.op(eng, lambda e, o=out, i=in0, s=scalar, j=in1, p=op0, q=op1:
                       e.scalar_tensor_tensor(o, i, s, j, p, q), wait, sig,
                       osz=self._osz(out))

    def act(self, out, in_, func, bias=0.0, scale=1.0, accum=None, wait=(), sig="sc"):
        oz = 1 if accum is not None else self._osz(out)
        return self.op("scalar",
                       lambda e, o=out, i=in_, f=func, b=bias, sc=scale, ac=accum:
                       e.activation(o, i, f, bias=b, scale=sc, accum_out=ac),
                       wait, sig, osz=oz)

    def ttr(self, out, in0, in1, scale, accum, wait=(), sig="ve"):
        return self.op("vector",
                       lambda e, o=out, a=in0, b=in1, s=scale, ac=accum:
                       e.tensor_tensor_reduce(o, a, b, s, 0.0, ALU.mult, ALU.add,
                                              accum_out=ac),
                       wait, sig, osz=1)

    def red(self, out, in_, op, wait=(), sig="ve", negate=None):
        return self.op("vector", lambda e, o=out, i=in_, p=op, n=negate:
                       e.tensor_reduce(o, i, AX.X, p, negate=n), wait, sig,
                       osz=self._osz(out))

    def cp(self, eng, out, in_, wait=(), sig=None):
        sig = sig or {"vector": "ve", "gpsimd": "gp", "scalar": "sc"}[eng]
        if eng == "scalar":
            return self.op(eng, lambda e, o=out, i=in_: e.copy(o, i), wait, sig,
                           osz=self._osz(out))
        return self.op(eng, lambda e, o=out, i=in_: e.tensor_copy(o, i), wait, sig,
                       osz=self._osz(out))

    def build(self, block):
        own_sem = {"vector": "ve", "scalar": "sc"}

        def mk(eng_name):
            osem = own_sem.get(eng_name)

            def run(eng):
                hwm = {}
                cnt = 0
                prev_small = False
                for waits, fn, inc, osz in self.ops[eng_name]:
                    merged = {}
                    for (s, v) in waits:
                        if v > merged.get(s, 0):
                            merged[s] = v
                    if osem is not None and prev_small and cnt > merged.get(osem, 0):
                        merged[osem] = cnt
                    for s, v in merged.items():
                        if hwm.get(s, 0) < v:
                            eng.wait_ge(self.sem[s], v)
                            hwm[s] = v
                    ins = fn(eng)
                    if inc is not None:
                        assert ins is not None, f"sig op returned None on {eng_name}"
                        ins.then_inc(self.sem[inc[0]], inc[1])
                        if inc[0] == osem:
                            cnt += inc[1]
                    prev_small = (osz is not None and osz < 256)
            return run
        block.tensor(mk("tensor"))
        block.vector(mk("vector"))
        block.scalar(mk("scalar"))
        block.gpsimd(mk("gpsimd"))
        block.sync(mk("sync"))


class Carver:

    def __init__(self, arena_ap, nbytes):
        self.ap = arena_ap
        self.nbytes = nbytes
        self.off = 0

    def take(self, n_elems, dtype):
        esz = 4 if dtype == F32 else 2
        start = (self.off + 63) & ~63
        nb = n_elems * esz
        assert start + nb <= self.nbytes, f"SBUF arena overflow: {start + nb} > {self.nbytes}"
        self.off = start + nb
        sl = self.ap[:, start // 2:(start + nb) // 2]
        if dtype == F32:
            sl = sl.bitcast(F32)
        return sl

    def f32(self, n):
        return self.take(n, F32)

    def bf16(self, n):
        return self.take(n, BF16)

    def reset(self):
        self.off = 0



_PHASES = 5

def build_nc(T=8192, taps=()):
    NTT = T // 128
    nc = bass.Bass("TRN2", target_bir_lowering=False, debug=False)

    x_h = nc.dram_tensor("x", [T, D], F32, kind="ExternalInput").ap()
    wqkv_h = nc.dram_tensor("wqkv", [128, 3 * DC * D], F32, kind="ExternalInput").ap()
    wgp_h = nc.dram_tensor("wgp", [128, 2 * DC * D], F32, kind="ExternalInput").ap()
    wmisc_h = nc.dram_tensor("wmisc", [128, DC * 17], F32, kind="ExternalInput").ap()
    wout_h = nc.dram_tensor("wout", [128, DC * D], F32, kind="ExternalInput").ap()
    wmlp1_h = nc.dram_tensor("wmlp1", [128, DC * DFF], F32, kind="ExternalInput").ap()
    wmlp2_h = nc.dram_tensor("wmlp2", [128, 32 * D], F32, kind="ExternalInput").ap()
    wmoe1_h = nc.dram_tensor("wmoe1", [128, DC * DFF], F32, kind="ExternalInput").ap()
    wmoe2_h = nc.dram_tensor("wmoe2", [128, 32 * D], F32, kind="ExternalInput").ap()
    b2cat_h = nc.dram_tensor("b2cat", [E, D], F32, kind="ExternalInput").ap()
    ppack_h = nc.dram_tensor("ppack", [128, 128], F32, kind="ExternalInput").ap()
    prow_h = nc.dram_tensor("prow", [1, 2072], F32, kind="ExternalInput").ap()
    cs_h = nc.dram_tensor("cs_tab", [T, D], F32, kind="ExternalInput").ap()
    cf32_h = nc.dram_tensor("cf32", [128, 257], F32, kind="ExternalInput").ap()
    cbf_h = nc.dram_tensor("cbf16", [128, 513], BF16, kind="ExternalInput").ap()

    def scr(name, shape, dtype):
        kind = "ExternalOutput" if name in taps else "Internal"
        return nc.dram_tensor(name, shape, dtype, kind=kind).ap()

    xnT_h = scr("xnT", [NTT // 2, 128, 2 * D], BF16)
    xn2T_h = scr("xn2T", [NTT, 128, D], BF16)
    tclip_h = scr("tclip", [NTT, 128, D], F32)
    pnat_h = scr("pnat", [NTT, 128, D], F32)
    qT_h = scr("qT", [NTT // 2, 128, 2 * D], BF16)
    kT_h = scr("kT", [NTT, 128, D], BF16)
    vnat_h = scr("vnat", [NTT, 128, D], BF16)
    x2T_h = scr("x2T", [NTT // 2, 128, 2 * D], BF16)
    gT_h = scr("gT", [NTT // 2, 128, 2 * D], BF16)
    selT_h = scr("selT", [E, T], BF16)
    pmaxrow_h = scr("pmaxrow", [1, D], F32)
    thrrow_h = scr("thrrow", [1, E], F32)
    dbg0_h = scr("dbg0", [128, 4 * 64 * 16], F32)
    dbg1_h = scr("dbg1", [NTT, 128, 16], F32)
    dbg2_h = scr("dbg2", [NTT, 128, D], F32)
    out_h = nc.dram_tensor("out", [T, D], F32, kind="ExternalOutput").ap()

    import contextlib
    es = contextlib.ExitStack()
    with es:
        ARENA_BYTES = 196 * 1024
        AR = es.enter_context(nc.sbuf_tensor("AR", [128, ARENA_BYTES // 2], BF16))
        PCtens = es.enter_context(nc.sbuf_tensor("PC", [128, 2048], F32))
        PBtens = es.enter_context(nc.sbuf_tensor("PB", [128, 1024], BF16))
        pb = [es.enter_context(nc.psum_tensor(f"pb{i}", [128, 512], F32)) for i in range(8)]
        em = Em(nc)
        for nm in ("te", "ve", "sc", "gp", "qs0", "qs1", "qs2", "qs3",
                   "qa0", "qa1", "qg", "qc", "qk0", "qk1", "qk2", "qk3",
                   "qb0", "qb1", "qv0", "qv1"):
            em.add_sem(nm, es.enter_context(nc.semaphore(nm)))
        block = es.enter_context(nc.Block())

        pcv = Carver(PCtens.ap().bitcast(BF16), 8192)
        ppack = pcv.f32(128)
        clock_all = pcv.f32(NTT)
        bg_bc = pcv.f32(16)
        cb_bc = pcv.f32(1)
        eps5 = pcv.f32(1)
        eps8 = pcv.f32(1)
        onet = pcv.f32(1)
        cf32 = pcv.f32(257)
        pbv = Carver(PBtens.ap(), 2048)
        cbf = pbv.bf16(513)

        LT = cf32[:, 0:128]
        IDF = cf32[:, 128:256]
        ONESC = cf32[:, 256:257]
        ONESR = cf32[0:1, 0:128]
        ONESC_B = cbf[:, 512:513]
        ONESR_B = cbf[0:1, 0:128]

        def pcol(base, n):
            return [ppack[:, base + c:base + c + 1] for c in range(n)]
        ln1g_c, ln1b_c = pcol(0, 8), pcol(8, 8)
        ln2g_c, ln2b_c = pcol(16, 8), pcol(24, 8)
        gam_c, bet_c = pcol(32, 8), pcol(40, 8)
        outb_c, mlpb2_c = pcol(48, 8), pcol(56, 8)
        mlpb1_c, b1cat_c = pcol(64, 32), pcol(96, 32)

        tfix = []
        tfix.append(em.dma("sync", ppack, ppack_h, sem="qc"))
        tfix.append(em.dma("sync", cf32, cf32_h, sem="qc"))
        tfix.append(em.dma("sync", cbf, cbf_h, sem="qc"))
        tfix.append(em.dma("sync", bg_bc, prow_h[:, 2048:2064].broadcast_to([128, 16]), sem="qc"))
        tfix.append(em.dma("sync", cb_bc, prow_h[:, 2064:2065].broadcast_to([128, 1]), sem="qc"))
        tfix.append(em.dma("sync", eps5, prow_h[:, 2065:2066].broadcast_to([128, 1]), sem="qc"))
        tfix.append(em.dma("sync", eps8, prow_h[:, 2066:2067].broadcast_to([128, 1]), sem="qc"))
        tfix.append(em.dma("sync", onet, prow_h[:, 2067:2068].broadcast_to([128, 1]), sem="qc"))

        ctx = dict(locals())
        ctx["taps"] = taps

        _phase0(ctx, Carver(AR.ap(), ARENA_BYTES))
        em.barrier()
        _phase1(ctx, Carver(AR.ap(), ARENA_BYTES))
        em.barrier()
        if _PHASES >= 3:
            _phase2(ctx, Carver(AR.ap(), ARENA_BYTES))
            em.barrier()
        if _PHASES >= 4:
            _phase3(ctx, Carver(AR.ap(), ARENA_BYTES))
            em.barrier()
        if _PHASES >= 5:
            _phase4(ctx, Carver(AR.ap(), ARENA_BYTES))

        snap = {s: em.cnt[s] for s in em.cnt}
        w = [(s, v) for s, v in snap.items() if v > 0]
        em.op("sync", lambda e: None, wait=w)
        em.build(block)
    return nc


def _phase0(c, cv):
    nc, em, pb, T, NTT = c["nc"], c["em"], c["pb"], c["T"], c["NTT"]
    LT, IDF, ONESC, ONESR = c["LT"], c["IDF"], c["ONESC"], c["ONESR"]
    eps5, eps8, onet = c["eps5"], c["eps8"], c["onet"]
    x_h, wgp_h, wmisc_h = c["x_h"], c["wgp_h"], c["wmisc_h"]
    xnT_h, xn2T_h, tclip_h, pnat_h = c["xnT_h"], c["xn2T_h"], c["tclip_h"], c["pnat_h"]
    selT_h, pmaxrow_h, thrrow_h, prow_h = c["selT_h"], c["pmaxrow_h"], c["thrrow_h"], c["prow_h"]
    ln1g_c, ln1b_c = c["ln1g_c"], c["ln1b_c"]
    clock_all, bg_bc, cb_bc = c["clock_all"], c["bg_bc"], c["cb_bc"]

    xf = [cv.f32(D) for _ in range(4)]
    xn = [cv.f32(D), cv.f32(D)]
    xn2f = [cv.f32(D), cv.f32(D)]
    sqd = cv.f32(D)
    glf = [cv.f32(D), cv.f32(D)]
    ta = [cv.f32(D), cv.f32(D)]
    tb = [cv.f32(D), cv.f32(D)]
    spf = [cv.f32(D), cv.f32(D)]
    tclipf = [cv.f32(D), cv.f32(D)]
    pnf = [cv.f32(D), cv.f32(D)]
    pmax_acc = cv.f32(D)
    carry = cv.f32(D)[0:1, :]
    czcarry = cv.f32(16)[0:1, :]
    lgf = [cv.f32(16), cv.f32(16)]
    mk1, mk2, lg2 = cv.f32(16), cv.f32(16), cv.f32(16)
    sm = [cv.f32(16) for _ in range(4)]
    mask_all = cv.f32(NTT * 16)
    cumz_all = cv.f32(NTT * 16)
    ind_all = cv.f32(NTT * 16)
    zind_all = cv.f32(NTT * 16)
    thr_bc = cv.f32(16)
    pmr = cv.f32(16)
    pmst = cv.f32(128)
    thrr = cv.f32(16)[0:1, :]
    stgA = [cv.bf16(D), cv.bf16(D)]
    stgB = [cv.bf16(D), cv.bf16(D)]
    selTst = cv.bf16(T)
    wgp_sb = cv.bf16(2 * DC * D)
    wmisc_sb = cv.bf16(DC * 17)
    gb_bc = cv.f32(D)
    pb_bc = cv.f32(D)

    twg = em.dma("gpsimd", wgp_sb, wgp_h)
    twm = em.dma("gpsimd", wmisc_sb, wmisc_h)
    tgb = em.dma("sync", gb_bc, prow_h[:, 0:1024].broadcast_to([128, 1024]), sem="qc")
    tpbc = em.dma("sync", pb_bc, prow_h[:, 1024:2048].broadcast_to([128, 1024]), sem="qc")
    tmc = em.v(lambda e: e.memset(carry, 0.0))
    tmz = em.v(lambda e: e.memset(czcarry, 0.0))
    wconsts = c["tfix"] + [twg, twm, tgb, tpbc, tmc, tmz]
    for eng in ENGINES:
        em.op(eng, lambda e: None, wait=wconsts)

    xf_free = [None] * 4
    xn_lastT = [None, None]
    xn2_lastT = [None, None]
    stg_free = [[], []]
    glf_free = [None, None]
    sp_free = [None, None]
    pn_free = [[], []]
    tclip_free = [None, None]
    lgf_free = [None, None]
    bank_free = [None] * 8
    t_carryupd = None
    t_czupd = None
    t_pmax = None

    x_t = x_h.rearrange("(n p) d -> n p d", p=128)
    htok = {}

    def head(i):
        sl = i % 2
        fl = i % 4
        S = sm[fl]
        s0, s1, s2, s3 = S[:, 0:1], S[:, 1:2], S[:, 2:3], S[:, 3:4]
        s4, s5, s6 = S[:, 4:5], S[:, 5:6], S[:, 6:7]
        s7, s8, s9 = S[:, 7:8], S[:, 8:9], S[:, 9:10]
        s10, s11, s12, s13 = S[:, 10:11], S[:, 11:12], S[:, 12:13], S[:, 13:14]
        tL = em.dma("sync", xf[fl], x_t[i], sem=f"qs{fl}",
                    wait=[xf_free[fl]] if xf_free[fl] else [])
        tsum = em.red(s0, xf[fl], ALU.add, wait=[tL])
        tsq = em.act(sqd, xf[fl], AF.Square, accum=s2, wait=[tL])
        em.ts("vector", s1, s0, 1.0 / D, None, ALU.mult)
        em.ts("vector", s3, s2, 1.0 / D, None, ALU.mult, wait=[tsq])
        em.tt("vector", s4, s1, s1, ALU.mult)
        tvar = em.tt("vector", s4, s3, s4, ALU.subtract)
        tln = em.act(s5, s4, AF.Ln, bias=eps5, wait=[tvar])
        trstd = em.act(s6, s5, AF.Exp, scale=-0.5, wait=[tln])
        wxn = [trstd] + ([xn_lastT[sl]] if xn_lastT[sl] else [])
        txn = em.ts("vector", xn[sl], xf[fl], s1, s6, ALU.subtract, ALU.mult, wait=wxn)
        xf_free[fl] = txn
        tsum1 = em.red(s7, xn[sl], ALU.add, wait=[txn])
        tsq1 = em.act(sqd, xn[sl], AF.Square, accum=s8, wait=[txn])
        em.ts("vector", s9, s7, 1.0 / D, None, ALU.mult)
        em.ts("vector", s10, s8, 1.0 / D, None, ALU.mult, wait=[tsq1])
        em.tt("vector", s11, s9, s9, ALU.mult)
        tvar1 = em.tt("vector", s11, s10, s11, ALU.subtract)
        tln1 = em.act(s12, s11, AF.Ln, bias=eps5, wait=[tvar1])
        trstd1 = em.act(s13, s12, AF.Exp, scale=-0.5, wait=[tln1])
        wxn2 = [trstd1] + ([xn2_lastT[sl]] if xn2_lastT[sl] else [])
        txn2 = em.ts("vector", xn2f[sl], xn[sl], s9, s13, ALU.subtract, ALU.mult,
                     wait=wxn2)
        htok[i] = (txn, txn2)

    ttok = {}

    def tail_a(i):
        sl = i % 2
        txn, txn2 = htok.pop(i)

        tA = tB = None
        tt1_last = tt2_last = None
        evA = {}
        evB = {}
        for dc in range(DC):
            b0 = dc % 2
            b1 = 2 + dc % 2
            wA = [txn] + (stg_free[sl] if dc == 0 else [])
            if dc >= 2:
                wA.append(evA[dc - 2])
            elif bank_free[b0]:
                wA.append(bank_free[b0])
            tt1 = em.transp(pb[b0][:, 0:128], xn[sl][:, dc * 128:(dc + 1) * 128],
                            IDF, wait=wA)
            tA = em.cp("scalar", stgA[sl][:, dc * 128:(dc + 1) * 128],
                       pb[b0][:, 0:128], wait=[tt1])
            evA[dc] = tA
            bank_free[b0] = tA
            wB = [txn2]
            if dc >= 2:
                wB.append(evB[dc - 2])
            elif bank_free[b1]:
                wB.append(bank_free[b1])
            tt2 = em.transp(pb[b1][:, 0:128], xn2f[sl][:, dc * 128:(dc + 1) * 128],
                            IDF, wait=wB)
            tB = em.ts("vector", stgB[sl][:, dc * 128:(dc + 1) * 128],
                       pb[b1][:, 0:128], ln1g_c[dc], ln1b_c[dc],
                       ALU.mult, ALU.add, wait=[tt2])
            evB[dc] = tB
            bank_free[b1] = tB
            tt1_last, tt2_last = tt1, tt2
        xn_lastT[sl] = tt1_last
        xn2_lastT[sl] = tt2_last
        tdA = em.dma("scalar",
                     xnT_h[i // 2].rearrange("p (c u n) -> p c u n", c=DC, u=2)[:, :, i % 2, :],
                     stgA[sl].rearrange("p (c n) -> p c n", c=DC),
                     wait=[tA], sem=f"qa{sl}")
        tdB = em.dma("scalar", xn2T_h[i], stgB[sl], wait=[tB], sem=f"qa{sl}")
        ttok[i] = (tA, tB, evA, evB, tdB)

    def tail_b(i):
        nonlocal t_carryupd, t_czupd, t_pmax
        sl = i % 2
        S = sm[i % 4]
        s0, s2 = S[:, 0:1], S[:, 2:3]
        tA, tB, evA, evB, tdB = ttok.pop(i)

        wg_ = [tB] + [t for t in (bank_free[4], bank_free[5]) if t]
        wp_ = [tB] + [t for t in (bank_free[6], bank_free[7]) if t]
        wck = [tB, evA[6]]

        wlg = [tA, evA[7]]
        tgA = tgB_ = tpA = tpB_ = tckm = tlgm = None
        for dc in range(DC):
            lhsT = stgB[sl][:, dc * 128:(dc + 1) * 128]
            lhsTA = stgA[sl][:, dc * 128:(dc + 1) * 128]
            st_, sp_ = (dc == 0), (dc == DC - 1)
            tgA = em.mm(pb[4][:, :], lhsT, wgp_sb[:, dc * 1024:dc * 1024 + 512],
                        st_, sp_, wait=wg_ if dc == 0 else [])
            tgB_ = em.mm(pb[5][:, :], lhsT, wgp_sb[:, dc * 1024 + 512:(dc + 1) * 1024],
                         st_, sp_)
            tpA = em.mm(pb[6][:, :], lhsT,
                        wgp_sb[:, 8192 + dc * 1024:8192 + dc * 1024 + 512],
                        st_, sp_, wait=wp_ if dc == 0 else [])
            tpB_ = em.mm(pb[7][:, :], lhsT,
                         wgp_sb[:, 8192 + dc * 1024 + 512:8192 + (dc + 1) * 1024],
                         st_, sp_)
            tckm = em.mm(pb[0][:, 0:1], lhsT, wmisc_sb[:, dc * 17:dc * 17 + 1],
                         st_, sp_, wait=wck if dc == 0 else [])
            tlgm = em.mm(pb[1][:, 0:16], lhsTA, wmisc_sb[:, dc * 17 + 1:(dc + 1) * 17],
                         st_, sp_, wait=wlg if dc == 0 else [])


        wg0 = [tgB_] + ([glf_free[sl]] if glf_free[sl] else [])
        em.tt("vector", glf[sl][:, 0:512], pb[4][:, :], gb_bc[:, 0:512], ALU.add,
              wait=wg0)
        tgl = em.tt("vector", glf[sl][:, 512:1024], pb[5][:, :], gb_bc[:, 512:1024],
                    ALU.add)
        wsp = [tgl] + ([sp_free[sl]] if sp_free[sl] else [])
        tab = em.act(ta[sl], glf[sl], AF.Abs, wait=wsp)
        tex = em.act(tb[sl], ta[sl], AF.Exp, scale=-1.0, wait=[tab])
        tlnsp = em.act(ta[sl], tb[sl], AF.Ln, bias=onet, wait=[tex])
        trel = em.ts("vector", tb[sl], glf[sl], 0.0, None, ALU.max, wait=[tlnsp])
        glf_free[sl] = trel
        tsp = em.tt("vector", spf[sl], ta[sl], tb[sl], ALU.add, wait=[tlnsp, trel])

        wp0 = [tpB_] + pn_free[sl]
        em.tt("vector", pnf[sl][:, 0:512], pb[6][:, :], pb_bc[:, 0:512], ALU.add,
              wait=wp0)
        tpn = em.tt("vector", pnf[sl][:, 512:1024], pb[7][:, :], pb_bc[:, 512:1024],
                    ALU.add)
        tpd = em.dma("scalar", pnat_h[i], pnf[sl], wait=[tpn], sem=f"qb{sl}")
        del tpd
        if i == 0:
            t_pmax = em.cp("vector", pmax_acc, pnf[sl], wait=[tpn])
        else:
            t_pmax = em.tt("vector", pmax_acc, pmax_acc, pnf[sl], ALU.max,
                           wait=[tpn, t_pmax])

        cl0, cl1 = S[:, 14:15], S[:, 15:16]
        tc0 = em.tt("vector", cl0, pb[0][:, 0:1], cb_bc, ALU.add, wait=[tckm])
        bank_free[0] = tc0
        tc1 = em.act(cl1, cl0, AF.Abs, wait=[tc0])
        tc2 = em.act(cl1, cl1, AF.Exp, scale=-1.0, wait=[tc1])
        tc3 = em.act(cl1, cl1, AF.Ln, bias=onet, wait=[tc2])
        tc4 = em.ts("vector", cl0, cl0, 0.0, None, ALU.max, wait=[tc3])
        em.tt("vector", clock_all[:, i:i + 1], cl0, cl1, ALU.add, wait=[tc4])

        wl0 = [tlgm] + ([lgf_free[sl]] if lgf_free[sl] else [])
        tlg0 = em.tt("vector", lgf[sl], pb[1][:, 0:16], bg_bc, ALU.add, wait=wl0)
        bank_free[1] = tlg0
        trm = em.red(s0, lgf[sl], ALU.max, wait=[tlg0])
        tm1 = em.ts("vector", mk1, lgf[sl], s0, None, ALU.is_ge, wait=[trm])
        tl2 = em.stt("vector", lg2, mk1, -1e30, lgf[sl], ALU.mult, ALU.add, wait=[tm1])
        lgf_free[sl] = tl2
        trm2 = em.red(s2, lg2, ALU.max, wait=[tl2])
        tm2 = em.ts("vector", mk2, lg2, s2, None, ALU.is_ge, wait=[trm2])
        msk_sl = mask_all[:, i * 16:(i + 1) * 16]
        tmask = em.tt("vector", msk_sl, mk1, mk2, ALU.add, wait=[tm2])
        zi_sl = zind_all[:, i * 16:(i + 1) * 16]
        tzi = em.ts("vector", zi_sl, msk_sl, -1.0, 1.0, ALU.mult, ALU.add, wait=[tmask])

        wc = [tsp, tpn] + ([t_carryupd] if t_carryupd else [])
        em.mm(pb[6][:, :], ONESR, carry[:, 0:512], True, False, wait=wc)
        em.mm(pb[7][:, :], ONESR, carry[:, 512:1024], True, False)
        em.mm(pb[6][:, :], LT, spf[sl][:, 0:512], False, True)
        tcs = em.mm(pb[7][:, :], LT, spf[sl][:, 512:1024], False, True)
        wcs = [tsp, tB]
        em.mm(pb[2][0:1, 0:512], ONESC, spf[sl][:, 0:512], True, True, wait=wcs)
        tcol = em.mm(pb[3][0:1, 0:512], ONESC, spf[sl][:, 512:1024], True, True)
        wtc = [tcs] + ([tclip_free[sl]] if tclip_free[sl] else [])
        em.ts("vector", tclipf[sl][:, 0:512], pb[6][:, :], 60.0, None, ALU.min,
              wait=wtc)
        ttc = em.ts("vector", tclipf[sl][:, 512:1024], pb[7][:, :], 60.0, None, ALU.min)
        bank_free[6] = ttc
        bank_free[7] = ttc
        ttd = em.dma("scalar", tclip_h[i], tclipf[sl], wait=[ttc], sem=f"qb{sl}")
        tclip_free[sl] = ttd
        stg_free[sl] = [tlgm, tdB]
        pn_free[sl] = [t_pmax, ttd]
        em.tt("vector", carry[:, 0:512], carry[:, 0:512], pb[2][0:1, 0:512], ALU.add,
              wait=[tcol])
        t_carryupd = em.tt("vector", carry[:, 512:1024], carry[:, 512:1024],
                           pb[3][0:1, 0:512], ALU.add)
        bank_free[2] = t_carryupd
        bank_free[3] = t_carryupd

        wz = [tzi, tgl] + ([t_czupd] if t_czupd else [])
        em.mm(pb[4][:, 0:16], ONESR, czcarry, True, False, wait=wz)
        tcz = em.mm(pb[4][:, 0:16], LT, zi_sl, False, True)
        tczs = em.mm(pb[5][0:1, 0:16], ONESC, zi_sl, True, True)
        tcza = em.cp("vector", cumz_all[:, i * 16:(i + 1) * 16], pb[4][:, 0:16],
                     wait=[tcz])
        t_czupd = em.tt("vector", czcarry, czcarry, pb[5][0:1, 0:16], ALU.add,
                        wait=[tczs])
        bank_free[4] = tcza
        bank_free[5] = t_czupd

    head(0)
    for i in range(NTT):
        tail_a(i)
        if i + 1 < NTT:
            head(i + 1)
        tail_b(i)

    tr = None
    for dc in range(DC):
        b = dc % 2
        w = [t_pmax] + ([bank_free[b]] if bank_free[b] else [])
        tt1 = em.transp(pb[b][:, 0:128], pmax_acc[:, dc * 128:(dc + 1) * 128], IDF,
                        wait=w)
        tr = em.red(pmr[:, dc:dc + 1], pb[b][:, 0:128], ALU.max, wait=[tt1])
        bank_free[b] = tr
    w = [tr] + ([bank_free[2]] if bank_free[2] else [])
    tt2 = em.transp(pb[2][0:8, 0:128], pmr[:, 0:8], IDF, wait=w)
    tpc = em.cp("vector", pmst[0:8, :], pb[2][0:8, 0:128], wait=[tt2])
    bank_free[2] = tpc
    em.dma("scalar", pmaxrow_h.rearrange("o (a b) -> (o a) b", a=8), pmst[0:8, :],
           wait=[tpc], sem="qa0")

    tth = em.ts("vector", thrr, czcarry, float(T // 2), None, ALU.subtract,
                wait=[t_czupd])
    tthd = em.dma("scalar", thrrow_h, thrr, wait=[tth], sem="qa0")
    tthb = em.dma("sync", thr_bc, thrrow_h.broadcast_to([128, 16]), wait=[tthd],
                  sem="qc")

    tlast = None
    for i in range(NTT):
        tlast = em.tt("vector", ind_all[:, i * 16:(i + 1) * 16],
                      cumz_all[:, i * 16:(i + 1) * 16], thr_bc, ALU.is_le,
                      wait=[tthb] if i == 0 else [])
    em.tt("vector", ind_all, zind_all, ind_all, ALU.mult, wait=[tlast])
    tsel = em.tt("vector", ind_all, mask_all, ind_all, ALU.add)
    tse = None
    for i in range(NTT):
        b = 3 + (i % 2)
        w = [tsel] + ([bank_free[b]] if bank_free[b] else [])
        tt3 = em.transp(pb[b][0:16, 0:128], ind_all[:, i * 16:(i + 1) * 16], IDF,
                        wait=w)
        tse = em.cp("vector", selTst[0:16, i * 128:(i + 1) * 128], pb[b][0:16, 0:128],
                    wait=[tt3])
        bank_free[b] = tse
    em.dma("scalar", selT_h, selTst[0:16, :], wait=[tse], sem="qa0")
    if "dbg0" in c["taps"]:
        em.dma("scalar", c["dbg0_h"][:, 0:NTT * 16], mask_all, wait=[tsel], sem="qa0")
        em.dma("scalar", c["dbg0_h"][:, 1024:1024 + NTT * 16], cumz_all, wait=[tsel], sem="qa0")
        em.dma("scalar", c["dbg0_h"][:, 2048:2048 + NTT * 16], ind_all, wait=[tsel], sem="qa0")
        em.dma("scalar", c["dbg0_h"][:, 3072:3072 + 16], thr_bc[:, 0:16], wait=[tsel], sem="qa0")


def _phase1(c, cv):
    nc, em, pb, T, NTT = c["nc"], c["em"], c["pb"], c["T"], c["NTT"]
    LT, IDF, ONESC, ONESR = c["LT"], c["IDF"], c["ONESC"], c["ONESR"]
    eps8 = c["eps8"]
    wqkv_h, cs_h = c["wqkv_h"], c["cs_h"]
    xn2T_h, tclip_h, pnat_h = c["xn2T_h"], c["tclip_h"], c["pnat_h"]
    qT_h, kT_h, vnat_h, pmaxrow_h = c["qT_h"], c["kT_h"], c["vnat_h"], c["pmaxrow_h"]
    clock_all = c["clock_all"]

    stg2 = [cv.bf16(D) for _ in range(4)]
    tcl = [cv.f32(D) for _ in range(4)]
    pnl = [cv.f32(D) for _ in range(4)]
    cst = [cv.f32(D), cv.f32(D)]
    vbf = [cv.bf16(D), cv.bf16(D)]
    tqf = [cv.f32(D), cv.f32(D)]
    tkf = [cv.f32(D), cv.f32(D)]
    qrof = [cv.f32(D), cv.f32(D)]
    krof = [cv.f32(D), cv.f32(D)]
    pexf = [cv.f32(D), cv.f32(D)]
    lnpf = [cv.f32(D), cv.f32(D)]
    uf = [cv.f32(D), cv.f32(D)]
    gjf = [cv.f32(D), cv.f32(D)]
    e1 = [cv.f32(512), cv.f32(512)]
    o1 = [cv.f32(512), cv.f32(512)]
    sqd = cv.f32(D)
    sm = [cv.f32(16), cv.f32(16)]
    stgQ = [cv.bf16(D), cv.bf16(D)]
    stgK = [cv.bf16(D), cv.bf16(D)]
    carryP = cv.f32(D)[0:1, :]
    pmax_bc = cv.f32(D)
    qkv_sb = cv.bf16(3 * DC * D)

    twq = em.dma("gpsimd", qkv_sb, wqkv_h)
    tpm = em.dma("sync", pmax_bc, pmaxrow_h.broadcast_to([128, 1024]), sem="qc")
    tmc = em.v(lambda e: e.memset(carryP, 0.0))

    cs_t = cs_h.rearrange("(n p) d -> n p d", p=128)

    stg2_free = [None] * 4
    tcl_free = [[] for _ in range(4)]
    pnl_free = [None] * 4
    cst_free = [None, None]
    buf_free = {}
    bank_free = [None] * 8
    stgq_free = [None, None]
    stgk_free = [None, None]
    t_carryP = tmc
    for sl in range(2):
        for nm in ("tqf", "tkf", "qrof", "krof", "pexf", "lnpf", "uf", "gjf",
                   "vbf"):
            buf_free[(nm, sl)] = None

    for i in range(NTT):
        sl = i % 2
        fl = i % 4
        S = sm[sl]
        qs0, qs1, qss = S[:, 0:1], S[:, 1:2], S[:, 2:3]
        rnq = S[:, 3:4]
        ks0, ks1, kss = S[:, 4:5], S[:, 5:6], S[:, 6:7]
        rnk = S[:, 7:8]

        w0 = [twq, tpm] if i == 0 else []
        em.dma("sync", stg2[fl], xn2T_h[i], sem=f"qs{fl}",
               wait=w0 + ([stg2_free[fl]] if stg2_free[fl] else []))
        em.dma("sync", tcl[fl], tclip_h[i], wait=tcl_free[fl], sem=f"qs{fl}")
        em.dma("sync", pnl[fl], pnat_h[i], sem=f"qs{fl}",
               wait=[pnl_free[fl]] if pnl_free[fl] else [])
        tls = em.dma("sync", cst[sl], cs_t[i], sem=f"qs{fl}",
                     wait=[cst_free[sl]] if cst_free[sl] else [])
        tl2 = tlc = tlp = tls

        tq_mm = tk_mm = tv_mm = None
        for dc in range(DC):
            lhsT = stg2[fl][:, dc * 128:(dc + 1) * 128]
            st_, sp_ = (dc == 0), (dc == DC - 1)
            wq_ = ([tl2] + [t for t in (bank_free[0], bank_free[1]) if t]) if dc == 0 else []
            wk_ = ([tl2] + [t for t in (bank_free[2], bank_free[3]) if t]) if dc == 0 else []
            wv_ = ([tl2] + [t for t in (bank_free[4], bank_free[5]) if t]) if dc == 0 else []
            em.mm(pb[0][:, :], lhsT, qkv_sb[:, dc * 1024:dc * 1024 + 512], st_, sp_,
                  wait=wq_)
            tq_mm = em.mm(pb[1][:, :], lhsT, qkv_sb[:, dc * 1024 + 512:(dc + 1) * 1024],
                          st_, sp_)
            em.mm(pb[2][:, :], lhsT, qkv_sb[:, 8192 + dc * 1024:8192 + dc * 1024 + 512],
                  st_, sp_, wait=wk_)
            tk_mm = em.mm(pb[3][:, :],
                          lhsT, qkv_sb[:, 8192 + dc * 1024 + 512:8192 + (dc + 1) * 1024],
                          st_, sp_)
            em.mm(pb[4][:, :], lhsT,
                  qkv_sb[:, 16384 + dc * 1024:16384 + dc * 1024 + 512], st_, sp_,
                  wait=wv_)
            tv_mm = em.mm(pb[5][:, :],
                          lhsT, qkv_sb[:, 16384 + dc * 1024 + 512:16384 + (dc + 1) * 1024],
                          st_, sp_)
        stg2_free[fl] = tv_mm

        wv0 = [tv_mm] + ([buf_free[("vbf", sl)]] if buf_free[("vbf", sl)] else [])
        em.cp("vector", vbf[sl][:, 0:512], pb[4][:, :], wait=wv0)
        tvb = em.cp("vector", vbf[sl][:, 512:1024], pb[5][:, :])
        tvd = em.dma("scalar", vnat_h[i], vbf[sl], wait=[tvb], sem=f"qv{sl}")

        tqs0 = em.act(sqd[:, 0:512], pb[0][:, :], AF.Square, accum=qs0, wait=[tq_mm])
        tqs1 = em.act(sqd[:, 512:1024], pb[1][:, :], AF.Square, accum=qs1, wait=[tq_mm])
        em.tt("vector", qss, qs0, qs1, ALU.add, wait=[tqs1])
        t3 = em.ts("vector", qss, qss, 1e-24, None, ALU.max)
        t4 = em.act(qss, qss, AF.Ln, wait=[t3])
        t5 = em.act(rnq, qss, AF.Exp, scale=-0.5, wait=[t4])
        wtq = [t5] + ([buf_free[("tqf", sl)]] if buf_free[("tqf", sl)] else [])
        em.ts("vector", tqf[sl][:, 0:512], pb[0][:, :], rnq, None, ALU.mult, wait=wtq)
        ttq = em.ts("vector", tqf[sl][:, 512:1024], pb[1][:, :], rnq, None, ALU.mult)
        bank_free[0] = ttq
        bank_free[1] = ttq
        tks0 = em.act(sqd[:, 0:512], pb[2][:, :], AF.Square, accum=ks0, wait=[tk_mm])
        tks1 = em.act(sqd[:, 512:1024], pb[3][:, :], AF.Square, accum=ks1, wait=[tk_mm])
        em.tt("vector", kss, ks0, ks1, ALU.add, wait=[tks1])
        t3 = em.ts("vector", kss, kss, 1e-24, None, ALU.max)
        t4 = em.act(kss, kss, AF.Ln, wait=[t3])
        t5 = em.act(rnk, kss, AF.Exp, scale=-0.5, wait=[t4])
        wtk = [t5] + ([buf_free[("tkf", sl)]] if buf_free[("tkf", sl)] else [])
        em.ts("vector", tkf[sl][:, 0:512], pb[2][:, :], rnk, None, ALU.mult, wait=wtk)
        ttk = em.ts("vector", tkf[sl][:, 512:1024], pb[3][:, :], rnk, None, ALU.mult)
        bank_free[2] = ttk
        bank_free[3] = ttk

        def rope(src, dst, dst_free, pre):
            tev = src[:, 0:1024:2]
            tod = src[:, 1:1024:2]
            w = [tls, pre] + ([dst_free] if dst_free else [])
            ta_ = em.tt("vector", e1[sl], tev, cst[sl][:, 0:512], ALU.mult, wait=w)
            tb_ = em.tt("vector", o1[sl], tod, cst[sl][:, 512:1024], ALU.mult)
            em.tt("vector", dst[:, 0:1024:2], e1[sl], o1[sl], ALU.subtract)
            em.tt("vector", e1[sl], tev, cst[sl][:, 512:1024], ALU.mult)
            em.tt("vector", o1[sl], tod, cst[sl][:, 0:512], ALU.mult)
            return em.tt("vector", dst[:, 1:1024:2], e1[sl], o1[sl], ALU.add)

        tqro = rope(tqf[sl], qrof[sl], buf_free[("qrof", sl)], ttq)
        buf_free[("tqf", sl)] = tqro
        tkro = rope(tkf[sl], krof[sl], buf_free[("krof", sl)], ttk)
        buf_free[("tkf", sl)] = tkro

        wpe = [tlp] + ([buf_free[("pexf", sl)]] if buf_free[("pexf", sl)] else [])
        tpe0 = em.tt("vector", pexf[sl], pnl[fl], pmax_bc, ALU.subtract, wait=wpe)
        pnl_free[fl] = tpe0
        tpe1 = em.act(pexf[sl], pexf[sl], AF.Exp, wait=[tpe0])
        tpe = em.ts("vector", pexf[sl], pexf[sl], clock_all[:, i:i + 1], None,
                    ALU.mult, wait=[tpe1])

        wc = [tpe] + [t for t in (bank_free[6], bank_free[7], t_carryP) if t]
        em.mm(pb[6][:, :], ONESR, carryP[:, 0:512], True, False, wait=wc)
        em.mm(pb[7][:, :], ONESR, carryP[:, 512:1024], True, False)
        em.mm(pb[6][:, :], LT, pexf[sl][:, 0:512], False, True)
        tpc = em.mm(pb[7][:, :], LT, pexf[sl][:, 512:1024], False, True)
        em.mm(pb[0][0:1, 0:512], ONESC, pexf[sl][:, 0:512], True, True, wait=[ttq])
        tcol = em.mm(pb[1][0:1, 0:512], ONESC, pexf[sl][:, 512:1024], True, True)
        em.tt("vector", carryP[:, 0:512], carryP[:, 0:512], pb[0][0:1, 0:512],
              ALU.add, wait=[tcol])
        t_carryP = em.tt("vector", carryP[:, 512:1024], carryP[:, 512:1024],
                         pb[1][0:1, 0:512], ALU.add)
        bank_free[0] = t_carryP
        bank_free[1] = t_carryP

        wlnp = [tpc] + ([buf_free[("lnpf", sl)]] if buf_free[("lnpf", sl)] else [])
        em.act(lnpf[sl][:, 0:512], pb[6][:, :], AF.Ln, bias=eps8, wait=wlnp)
        tlnp = em.act(lnpf[sl][:, 512:1024], pb[7][:, :], AF.Ln, bias=eps8)
        bank_free[6] = tlnp
        bank_free[7] = tlnp
        wu = [tlnp, tlc] + ([buf_free[("uf", sl)]] if buf_free[("uf", sl)] else [])
        tu0 = em.stt("vector", uf[sl], tcl[fl], -1.0, lnpf[sl], ALU.mult,
                     ALU.subtract, wait=wu)
        buf_free[("lnpf", sl)] = tu0
        tu1 = em.act(uf[sl], uf[sl], AF.Exp, wait=[tu0])
        tqp = em.tt("vector", qrof[sl], qrof[sl], uf[sl], ALU.mult,
                    wait=[tu1, tqro])
        buf_free[("uf", sl)] = tqp

        wgj = [tlc] + ([buf_free[("gjf", sl)]] if buf_free[("gjf", sl)] else [])
        tg0 = em.act(gjf[sl], tcl[fl], AF.Exp, scale=-1.0, wait=wgj)
        tcl_free[fl] = [tu0, tg0]
        tg1 = em.act(gjf[sl], gjf[sl], AF.Ln, bias=eps8, wait=[tg0])
        tg2 = em.act(gjf[sl], gjf[sl], AF.Exp, scale=-1.0, wait=[tg1])
        tk1 = em.tt("vector", krof[sl], krof[sl], pexf[sl], ALU.mult,
                    wait=[tkro, tpe])
        buf_free[("pexf", sl)] = tk1
        tkp = em.tt("vector", krof[sl], krof[sl], gjf[sl], ALU.mult, wait=[tg2])
        buf_free[("gjf", sl)] = tkp

        tAq = tAk = None
        ttq_last = ttk_last = None
        for dc in range(DC):
            bq = 2 + dc % 2
            wq = [tqp] + ([bank_free[bq]] if bank_free[bq] else []) \
                + ([stgq_free[sl]] if (dc == 0 and stgq_free[sl]) else [])
            t6 = em.transp(pb[bq][:, 0:128], qrof[sl][:, dc * 128:(dc + 1) * 128],
                           IDF, wait=wq)
            tAq = em.cp("vector", stgQ[sl][:, dc * 128:(dc + 1) * 128],
                        pb[bq][:, 0:128], wait=[t6])
            bank_free[bq] = tAq
            bk = 4 + dc % 2
            wk = [tkp] + ([bank_free[bk]] if bank_free[bk] else []) \
                + ([stgk_free[sl]] if (dc == 0 and stgk_free[sl]) else [])
            t7 = em.transp(pb[bk][:, 0:128], krof[sl][:, dc * 128:(dc + 1) * 128],
                           IDF, wait=wk)
            tAk = em.act(stgK[sl][:, dc * 128:(dc + 1) * 128], pb[bk][:, 0:128],
                         AF.Identity, wait=[t7])
            bank_free[bk] = tAk
            ttq_last, ttk_last = t6, t7
        buf_free[("qrof", sl)] = ttq_last
        buf_free[("krof", sl)] = ttk_last
        cst_free[sl] = tkro
        em.dma("scalar",
               qT_h[i // 2].rearrange("p (c u n) -> p c u n", c=DC, u=2)[:, :, i % 2, :],
               stgQ[sl].rearrange("p (c n) -> p c n", c=DC),
               wait=[tAq], sem=f"qa{sl}")
        tdk = em.dma("scalar", kT_h[i], stgK[sl], wait=[tAk], sem=f"qa{sl}")
        stgq_free[sl] = tdk
        stgk_free[sl] = tdk
        buf_free[("vbf", sl)] = tvd



_NC_CACHE = {}


def _get_nc(T, taps=()):
    key = (T, tuple(taps), _PHASES)
    if key not in _NC_CACHE:
        _NC_CACHE[key] = build_nc(T, taps)
    return _NC_CACHE[key]


def kernel(**inputs):
    x = np.asarray(inputs["L0_attn_x"], dtype=np.float32)
    B, T, _ = x.shape
    shared = _host_inputs(inputs, T)
    nc = _get_nc(T)
    in_maps = []
    for b in range(B):
        m = dict(shared)
        m["x"] = np.ascontiguousarray(x[b])
        in_maps.append(m)
    res = run_bass_kernel_spmd(nc, in_maps, core_ids=list(range(B)))
    out = np.stack([np.asarray(r["out"], dtype=np.float32) for r in res.results])
    return out


def _phase2(c, cv):
    nc, em, pb, T, NTT = c["nc"], c["em"], c["pb"], c["T"], c["NTT"]
    IDF, ONESR = c["IDF"], c["ONESR"]
    qT_h, kT_h, vnat_h, xnT_h, x2T_h = (c["qT_h"], c["kT_h"], c["vnat_h"],
                                        c["xnT_h"], c["x2T_h"])
    wout_h, cbf = c["wout_h"], c["cbf"]
    outb_c = c["outb_c"]
    NQB = T // 256

    v_sb = cv.bf16(NTT * D)
    wout_sb = cv.bf16(DC * D)
    qblk = [cv.bf16(2 * D), cv.bf16(2 * D)]
    xnt2 = [cv.bf16(2 * D), cv.bf16(2 * D)]
    kbuf = [cv.bf16(D) for _ in range(4)]
    wsl = [cv.bf16(256), cv.bf16(256), cv.bf16(256)]
    attb = cv.bf16(2 * D)
    x2st = [cv.bf16(2 * D), cv.bf16(2 * D)]
    rbc = cv.f32(256)
    zr = cv.f32(256)[0:1, :]

    MASK0 = cbf[:, 0:256]
    MASK1 = cbf[:, 256:512]
    ONESC_B = cbf[:, 512:513]

    tw = em.dma("gpsimd", wout_sb, wout_h)
    tv = None
    for kt in range(NTT):
        tv = em.dma("sync", v_sb[:, kt * D:(kt + 1) * D], vnat_h[kt], sem="qc")
    w0 = [tw, tv]

    qblk_free = [None, None]
    kbuf_free = [None] * 4
    x2_free = [None, None]
    attb_free = None
    bank_free = [None] * 8
    SBANKS = (0, 1, 7)
    sbank_free = [None, None, None]
    zr_free = None
    rbc_free = None

    for qb in range(NQB):
        sl = qb % 2
        nkt = 2 * qb + 2
        wq = (w0 if qb == 0 else []) + ([qblk_free[sl]] if qblk_free[sl] else [])
        tq = em.dma("sync", qblk[sl], qT_h[qb], sem=f"qs{sl}", wait=wq)
        tx = em.dma("sync", xnt2[sl], xnT_h[qb], sem=f"qs{sl}",
                    wait=[x2_free[sl]] if x2_free[sl] else [])
        tq = tx

        t_exp = {}
        t_av = {}

        def emit_zav(kt):
            wZ = [t_exp[kt]]
            if qb > 0 and kt == 0:
                wZ += [t for t in (bank_free[6], bank_free[2], bank_free[3],
                                   bank_free[4], bank_free[5]) if t]
            em.mm(pb[6][0:1, 0:256], ONESC_B, wsl[kt % 3],
                  (kt == 0), (kt == nkt - 1), wait=wZ)
            tAV = None
            for dc in range(DC):
                tAV = em.mm(pb[2 + dc // 2][:, (dc % 2) * 256:(dc % 2) * 256 + 256],
                            v_sb[:, kt * D + dc * 128:kt * D + (dc + 1) * 128],
                            wsl[kt % 3],
                            (kt == 0 and dc % 2 == 0),
                            (kt == nkt - 1 and dc % 2 == 1))
            t_av[kt] = tAV

        for kt in range(nkt):
            kb = kt % 4
            wk = [t_av[kt - 4]] if kt - 4 in t_av else []
            if qb > 0 and kt < 4 and kbuf_free[kb]:
                wk = wk + [kbuf_free[kb]]
            tk = em.dma("sync", kbuf[kb], kT_h[kt], sem=f"qk{kb}", wait=wk)
            sbk = pb[SBANKS[kt % 3]]
            wS = [tk, tq]
            if kt - 3 in t_exp:
                wS.append(t_exp[kt - 3])
            elif kt < 3 and sbank_free[kt % 3]:
                wS.append(sbank_free[kt % 3])
            tS = None
            for dc in range(DC):
                tS = em.mm(sbk[:, 0:256], kbuf[kb][:, dc * 128:(dc + 1) * 128],
                           qblk[sl][:, dc * 256:(dc + 1) * 256],
                           (dc == 0), (dc == DC - 1), wait=wS if dc == 0 else [])
            wE = [tS] + ([t_av[kt - 3]] if kt - 3 in t_av else [])
            if kt >= nkt - 2:
                tcl_ = em.ts("vector", sbk[:, 0:256], sbk[:, 0:256], 1.0, None,
                             ALU.min, wait=wE)
                wE = [tcl_]
            tE = em.act(wsl[kt % 3], sbk[:, 0:256], AF.Exp, scale=1.0 / 32.0,
                        wait=wE)
            if kt >= nkt - 2:
                msk = MASK0 if kt == nkt - 2 else MASK1
                tE = em.tt("vector", wsl[kt % 3], wsl[kt % 3], msk, ALU.mult,
                           wait=[tE])
            t_exp[kt] = tE
            if kt >= 1:
                emit_zav(kt - 1)
        emit_zav(nkt - 1)
        for j_ in range(min(4, nkt)):
            kbuf_free[(nkt - 1 - j_) % 4] = t_av[nkt - 1 - j_]
        for j_ in range(min(3, nkt)):
            sbank_free[(nkt - 1 - j_) % 3] = t_exp[nkt - 1 - j_]
        qblk_free[sl] = tS

        wz1 = [t_av[nkt - 1]] + ([zr_free] if zr_free else [])
        tz0 = em.cp("vector", zr, pb[6][0:1, 0:256], wait=wz1)
        tz1 = em.v(lambda e, o=zr: e.reciprocal(o, o), wait=[tz0])
        bank_free[6] = tz1
        kt_last7 = max(k_ for k_ in range(nkt) if k_ % 3 == 2) if nkt >= 3 else None
        wR = [tz1]
        if kt_last7 is not None:
            wR.append(t_exp[kt_last7])
        if bank_free[7]:
            wR.append(bank_free[7])
        tR = em.mm(pb[7][:, 0:256], ONESR, zr, True, True, wait=wR)
        wrb = [tR] + ([rbc_free] if rbc_free else [])
        trb = em.cp("vector", rbc, pb[7][:, 0:256], wait=wrb)
        bank_free[7] = trb
        sbank_free[2] = trb
        tatt = None
        wA0 = [trb] + ([attb_free] if attb_free else [])
        for dc in range(DC):
            tatt = em.tt("vector", attb[:, dc * 256:(dc + 1) * 256],
                         pb[2 + dc // 2][:, (dc % 2) * 256:(dc % 2) * 256 + 256],
                         rbc, ALU.mult, wait=wA0 if dc == 0 else [])
        zr_free = tatt

        tX = {}
        for mc in range(DC):
            wX = [tatt] if mc == 0 else []
            for ki in range(DC):
                tXm = em.mm(pb[2 + mc // 2][:, (mc % 2) * 256:(mc % 2) * 256 + 256],
                            wout_sb[:, ki * D + mc * 128:ki * D + (mc + 1) * 128],
                            attb[:, ki * 256:(ki + 1) * 256],
                            (ki == 0 and mc % 2 == 0),
                            (ki == DC - 1 and mc % 2 == 1),
                            wait=wX if ki == 0 else [])
            tX[mc] = tXm
        attb_free = tX[DC - 1]
        tE2 = None
        for mc in range(DC):
            wE2 = [tX[mc | 1]] + ([x2_free[sl]] if (mc == 0 and x2_free[sl]) else [])
            tE2 = em.stt("vector", x2st[sl][:, mc * 256:(mc + 1) * 256],
                         pb[2 + mc // 2][:, (mc % 2) * 256:(mc % 2) * 256 + 256],
                         outb_c[mc], xnt2[sl][:, mc * 256:(mc + 1) * 256],
                         ALU.add, ALU.add, wait=wE2)
            if mc // 2 == 0:
                bank_free[2] = tE2
            elif mc // 2 == 1:
                bank_free[3] = tE2
            elif mc // 2 == 2:
                bank_free[4] = tE2
            else:
                bank_free[5] = tE2
        td = em.dma("scalar", x2T_h[qb], x2st[sl], wait=[tE2], sem=f"qa{sl}")
        x2_free[sl] = td


def _phase3(c, cv):
    nc, em, pb, T, NTT = c["nc"], c["em"], c["pb"], c["T"], c["NTT"]
    ONESC, ONESR = c["ONESC"], c["ONESR"]
    eps5 = c["eps5"]
    xnT_h, selT_h, gT_h = c["xnT_h"], c["selT_h"], c["gT_h"]
    wmoe1_h, wmoe2_h, b2cat_h = c["wmoe1_h"], c["wmoe2_h"], c["b2cat_h"]
    gam_c, bet_c, b1cat_c = c["gam_c"], c["bet_c"], c["b1cat_c"]
    NQB = T // 256

    w1_sb = cv.bf16(DC * DFF)
    w2_sb = cv.bf16(32 * D)
    b2_sb = cv.bf16(D)
    xnb = [cv.bf16(2 * D), cv.bf16(2 * D)]
    selb = [cv.bf16(16 * 256), cv.bf16(16 * 256)]
    selr = [cv.bf16(256), cv.bf16(256)]
    Hm = cv.bf16(32 * 256)
    zf = cv.f32(2 * D)
    sqz = cv.f32(256)
    tmp = cv.f32(256)
    gst = [cv.bf16(2 * D), cv.bf16(2 * D)]
    mur = cv.f32(256)[0:1, :]
    rsr = cv.f32(256)[0:1, :]

    tw1 = em.dma("gpsimd", w1_sb, wmoe1_h)
    tw2 = em.dma("gpsimd", w2_sb, wmoe2_h)
    tb2 = em.dma("gpsimd", b2_sb[0:16, 0:D], b2cat_h)
    w0 = [tw1, tw2, tb2]

    xnb_free = [None, None]
    selb_free = [None, None]
    g_free = [None, None]
    Hm_free = None
    zf_free = None
    bank_free = [None] * 8
    row_free = None

    for j in range(NQB):
        sl = j % 2
        wx = (w0 if j == 0 else []) + ([xnb_free[sl]] if xnb_free[sl] else [])
        tlx = em.dma("sync", xnb[sl], xnT_h[j], sem=f"qs{sl}", wait=wx)
        wsel = [selb_free[sl]] if selb_free[sl] else []
        for e in range(E):
            em.dma("sync", selb[sl][:, e * 256:(e + 1) * 256],
                   selT_h[e:e + 1, j * 256:(j + 1) * 256].broadcast_to([128, 256]),
                   sem=f"qk{sl}", wait=wsel if e == 0 else [])
        tsel = em.dma("sync", selr[sl][0:16, :], selT_h[:, j * 256:(j + 1) * 256],
                      sem=f"qk{sl}")

        tgel = {}
        tmask = {}
        for mc in range(32):
            b = mc % 4
            wH = ([tlx] if mc == 0 else []) \
                + ([tmask[mc - 4]] if mc - 4 in tmask else []) \
                + ([bank_free[b]] if (mc < 4 and bank_free[b]) else [])
            tH = None
            for ki in range(DC):
                tH = em.mm(pb[b][:, 0:256],
                           w1_sb[:, ki * DFF + mc * 128:ki * DFF + (mc + 1) * 128],
                           xnb[sl][:, ki * 256:(ki + 1) * 256],
                           (ki == 0), (ki == DC - 1), wait=wH if ki == 0 else [])
            wG = [tH] + ([Hm_free] if (mc == 0 and Hm_free) else [])
            tg = em.act(Hm[:, mc * 256:(mc + 1) * 256], pb[b][:, 0:256], AF.Gelu,
                        bias=b1cat_c[mc], wait=wG)
            tgel[mc] = tg
            wM = [tg] + ([tsel] if mc == 0 else [])
            tmask[mc] = em.tt("vector", Hm[:, mc * 256:(mc + 1) * 256],
                              Hm[:, mc * 256:(mc + 1) * 256],
                              selb[sl][:, (mc // 2) * 256:(mc // 2 + 1) * 256],
                              ALU.mult, wait=wM)
        selb_free[sl] = tmask[31]
        t_lastmask = tmask[31]
        for b_ in range(4):
            bank_free[b_] = tgel[28 + b_]

        tz = None
        tz_prev = {}
        for mc in range(DC):
            b = 4 + mc % 2
            wO = [t_lastmask] if mc == 0 else []
            if mc < 2:
                if bank_free[b]:
                    wO.append(bank_free[b])
            else:
                wO.append(tz_prev[b])
            for ki in range(32):
                em.mm(pb[b][:, 0:256],
                      w2_sb[:, ki * D + mc * 128:ki * D + (mc + 1) * 128],
                      Hm[:, ki * 256:(ki + 1) * 256],
                      (ki == 0), False, wait=wO if ki == 0 else [])
            tO = em.mm(pb[b][:, 0:256], b2_sb[0:16, mc * 128:(mc + 1) * 128],
                       selr[sl][0:16, :], False, True)
            wz = [tO] + ([zf_free] if (mc == 0 and zf_free) else [])
            tz = em.tt("vector", zf[:, mc * 256:(mc + 1) * 256], pb[b][:, 0:256],
                       xnb[sl][:, mc * 256:(mc + 1) * 256], ALU.add, wait=wz)
            tz_prev[b] = tz
            bank_free[b] = tz
        Hm_free = tz
        xnb_free[sl] = tz

        wst = [tz] + ([bank_free[6]] if bank_free[6] else [])
        tmu = None
        for mc in range(DC):
            tmu = em.mm(pb[6][0:1, 0:256], ONESC, zf[:, mc * 256:(mc + 1) * 256],
                        (mc == 0), (mc == DC - 1), wait=wst if mc == 0 else [])
        wmu = [tmu] + ([row_free] if row_free else [])
        tmur = em.ts("vector", mur, pb[6][0:1, 0:256], 1.0 / D, None, ALU.mult,
                     wait=wmu)
        tsq_ = None
        for mc in range(DC):
            wsq = [tmur] if mc == 0 else [tsq_]
            ts0 = em.act(sqz, zf[:, mc * 256:(mc + 1) * 256], AF.Square, wait=wsq)
            tsq_ = em.mm(pb[6][0:1, 0:256], ONESC, sqz, (mc == 0), (mc == DC - 1),
                         wait=[ts0])
        tvm = em.ts("vector", rsr, pb[6][0:1, 0:256], 1.0 / D, None, ALU.mult,
                    wait=[tsq_])
        bank_free[6] = tvm
        tm2 = em.tt("vector", tmp[0:1, 0:256], mur, mur, ALU.mult, wait=[tmur])
        tvr = em.tt("vector", rsr, rsr, tmp[0:1, 0:256], ALU.subtract, wait=[tvm, tm2])
        tl_ = em.act(rsr, rsr, AF.Ln, bias=eps5[0:1, :], wait=[tvr])
        trs = em.act(rsr, rsr, AF.Exp, scale=-0.5, wait=[tl_])
        wb1 = [tmur, tvr] + ([bank_free[7]] if bank_free[7] else [])
        tbm = em.mm(pb[7][:, 0:256], ONESR, mur, True, True, wait=wb1)
        tbr = em.mm(pb[6][:, 0:256], ONESR, rsr, True, True, wait=[trs])
        row_free = tbr
        tgst = None
        for mc in range(DC):
            wn = [tbm, tbr]
            if mc == 0 and g_free[sl]:
                wn.append(g_free[sl])
            t1 = em.tt("vector", zf[:, mc * 256:(mc + 1) * 256],
                       zf[:, mc * 256:(mc + 1) * 256], pb[7][:, 0:256],
                       ALU.subtract, wait=wn)
            t2 = em.tt("vector", zf[:, mc * 256:(mc + 1) * 256],
                       zf[:, mc * 256:(mc + 1) * 256], pb[6][:, 0:256],
                       ALU.mult, wait=[t1])
            t3 = em.ts("vector", zf[:, mc * 256:(mc + 1) * 256],
                       zf[:, mc * 256:(mc + 1) * 256], gam_c[mc], bet_c[mc],
                       ALU.mult, ALU.add, wait=[t2])
            tgst = em.act(gst[sl][:, mc * 256:(mc + 1) * 256],
                          zf[:, mc * 256:(mc + 1) * 256], AF.Silu, wait=[t3])
        zf_free = tgst
        bank_free[7] = tgst
        bank_free[6] = tgst
        tgd = em.dma("scalar", gT_h[j], gst[sl], wait=[tgst], sem=f"qa{sl}")
        g_free[sl] = tgd


def _phase4(c, cv):
    nc, em, pb, T, NTT = c["nc"], c["em"], c["pb"], c["T"], c["NTT"]
    ONESC, ONESR, IDF = c["ONESC"], c["ONESR"], c["IDF"]
    eps5 = c["eps5"]
    x_h, out_h = c["x_h"], c["out_h"]
    x2T_h, gT_h = c["x2T_h"], c["gT_h"]
    wmlp1_h, wmlp2_h = c["wmlp1_h"], c["wmlp2_h"]
    ln2g_c, ln2b_c, mlpb1_c, mlpb2_c = (c["ln2g_c"], c["ln2b_c"],
                                        c["mlpb1_c"], c["mlpb2_c"])
    ONESC_B = c["cbf"][:, 512:513]
    NQB = T // 256

    w1_sb = cv.bf16(DC * DFF)
    w2_sb = cv.bf16(32 * D)
    x2b = [cv.bf16(2 * D), cv.bf16(2 * D)]
    gb2 = [cv.bf16(2 * D), cv.bf16(2 * D)]
    xb = cv.f32(2 * D)
    Hm = cv.bf16(32 * 256)
    hst = cv.bf16(2 * D)
    sf = cv.f32(2 * D)
    sqz = cv.f32(256)
    tmp = cv.f32(256)
    outt = [cv.f32(D), cv.f32(D)]
    mur = cv.f32(256)[0:1, :]
    rsr = cv.f32(256)[0:1, :]

    tw1 = em.dma("gpsimd", w1_sb, wmlp1_h)
    tw2 = em.dma("gpsimd", w2_sb, wmlp2_h)
    w0 = [tw1, tw2]

    x_t = x_h.rearrange("(n p) d -> n p d", p=128)
    out_t = out_h.rearrange("(n p) d -> n p d", p=128)

    x2_free = [None, None]
    g_free = [None, None]
    xb_free = None
    Hm_free = None
    hst_free = None
    sf_free = None
    outt_free = [None, None]
    bank_free = [None] * 8
    row_free = None

    for j in range(NQB):
        sl = j % 2
        wx = (w0 if j == 0 else []) + ([x2_free[sl]] if x2_free[sl] else [])
        em.dma("sync", x2b[sl], x2T_h[j], sem=f"qs{sl}", wait=wx)
        em.dma("sync", gb2[sl], gT_h[j], sem=f"qs{sl}",
               wait=[g_free[sl]] if g_free[sl] else [])
        em.dma("sync", xb[:, 0:D], x_t[2 * j], sem=f"qs{sl}",
               wait=[xb_free] if xb_free else [])
        tlx = em.dma("sync", xb[:, D:2 * D], x_t[2 * j + 1], sem=f"qs{sl}")

        wst = [tlx] + ([bank_free[6]] if bank_free[6] else [])
        tmu = None
        for mc in range(DC):
            tmu = em.mm(pb[6][0:1, 0:256], ONESC_B,
                        x2b[sl][:, mc * 256:(mc + 1) * 256],
                        (mc == 0), (mc == DC - 1), wait=wst if mc == 0 else [])
        wmu = [tmu] + ([row_free] if row_free else [])
        tmur = em.ts("vector", mur, pb[6][0:1, 0:256], 1.0 / D, None, ALU.mult,
                     wait=wmu)
        tsq_ = None
        for mc in range(DC):
            wsq = [tmur] if mc == 0 else [tsq_]
            ts0 = em.act(sqz, x2b[sl][:, mc * 256:(mc + 1) * 256], AF.Square,
                         wait=wsq)
            tsq_ = em.mm(pb[6][0:1, 0:256], ONESC, sqz, (mc == 0), (mc == DC - 1),
                         wait=[ts0])
        tvm = em.ts("vector", rsr, pb[6][0:1, 0:256], 1.0 / D, None, ALU.mult,
                    wait=[tsq_])
        bank_free[6] = tvm
        tm2 = em.tt("vector", tmp[0:1, 0:256], mur, mur, ALU.mult, wait=[tmur])
        tvr = em.tt("vector", rsr, rsr, tmp[0:1, 0:256], ALU.subtract,
                    wait=[tvm, tm2])
        tl_ = em.act(rsr, rsr, AF.Ln, bias=eps5[0:1, :], wait=[tvr])
        trs = em.act(rsr, rsr, AF.Exp, scale=-0.5, wait=[tl_])
        wb1 = [tmur, tvr] + ([bank_free[7]] if bank_free[7] else [])
        tbm = em.mm(pb[7][:, 0:256], ONESR, mur, True, True, wait=wb1)
        tbr = em.mm(pb[6][:, 0:256], ONESR, rsr, True, True, wait=[trs])
        row_free = tbr
        th_ = None
        for mc in range(DC):
            wn = [tbm, tbr]
            if mc == 0 and hst_free:
                wn.append(hst_free)
            t1 = em.tt("vector", sf[:, mc * 256:(mc + 1) * 256],
                       x2b[sl][:, mc * 256:(mc + 1) * 256], pb[7][:, 0:256],
                       ALU.subtract, wait=wn + ([sf_free] if (mc == 0 and sf_free) else []))
            t2 = em.tt("vector", sf[:, mc * 256:(mc + 1) * 256],
                       sf[:, mc * 256:(mc + 1) * 256], pb[6][:, 0:256],
                       ALU.mult, wait=[t1])
            th_ = em.ts("vector", hst[:, mc * 256:(mc + 1) * 256],
                        sf[:, mc * 256:(mc + 1) * 256], ln2g_c[mc], ln2b_c[mc],
                        ALU.mult, ALU.add, wait=[t2])
        bank_free[7] = th_
        bank_free[6] = th_

        tgel = {}
        for mc in range(32):
            b = mc % 4
            wH = ([th_] if mc == 0 else []) \
                + ([tgel[mc - 4]] if mc - 4 in tgel else []) \
                + ([bank_free[b]] if (mc < 4 and bank_free[b]) else [])
            tH = None
            for ki in range(DC):
                tH = em.mm(pb[b][:, 0:256],
                           w1_sb[:, ki * DFF + mc * 128:ki * DFF + (mc + 1) * 128],
                           hst[:, ki * 256:(ki + 1) * 256],
                           (ki == 0), (ki == DC - 1), wait=wH if ki == 0 else [])
            wG = [tH] + ([Hm_free] if (mc == 0 and Hm_free) else [])
            tgel[mc] = em.act(Hm[:, mc * 256:(mc + 1) * 256], pb[b][:, 0:256],
                              AF.Gelu, bias=mlpb1_c[mc], wait=wG)
        hst_free = tH
        t_lastH = tgel[31]
        bank_free[2] = tgel[30]
        bank_free[3] = tgel[31]

        tsg = None
        ts_prev = {}
        for mc in range(DC):
            b = 4 + mc % 2
            wO = [t_lastH] if mc == 0 else []
            if mc < 2:
                if bank_free[b]:
                    wO.append(bank_free[b])
            else:
                wO.append(ts_prev[b])
            tO = None
            for ki in range(32):
                tO = em.mm(pb[b][:, 0:256],
                           w2_sb[:, ki * D + mc * 128:ki * D + (mc + 1) * 128],
                           Hm[:, ki * 256:(ki + 1) * 256],
                           (ki == 0), (ki == 31), wait=wO if ki == 0 else [])
            ws_ = [tO]
            ts1 = em.stt("vector", sf[:, mc * 256:(mc + 1) * 256], pb[b][:, 0:256],
                         mlpb2_c[mc], x2b[sl][:, mc * 256:(mc + 1) * 256],
                         ALU.add, ALU.add, wait=ws_)
            ts_prev[b] = ts1
            bank_free[b] = ts1
            tsg = em.tt("vector", sf[:, mc * 256:(mc + 1) * 256],
                        sf[:, mc * 256:(mc + 1) * 256],
                        gb2[sl][:, mc * 256:(mc + 1) * 256], ALU.mult, wait=[ts1])
        sf_free_local = tsg
        x2_free[sl] = tsg
        g_free[sl] = tsg
        Hm_free = tO

        tev = None
        for par in range(2):
            for dc in range(DC):
                b = dc % 2
                wT = [tsg] + ([bank_free[b]] if bank_free[b] else []) \
                    + ([outt_free[par]] if (dc == 0 and outt_free[par]) else [])
                tt1 = em.transp(pb[b][:, 0:128],
                                sf[:, dc * 256 + par * 128:dc * 256 + par * 128 + 128],
                                IDF, wait=wT)
                tev = em.tt("vector", outt[par][:, dc * 128:(dc + 1) * 128],
                            pb[b][:, 0:128], xb[:, par * D + dc * 128:par * D + (dc + 1) * 128],
                            ALU.add, wait=[tt1])
                bank_free[b] = tev
            tod = em.dma("scalar", out_t[2 * j + par], outt[par], wait=[tev],
                         sem=f"qa{par}")
            outt_free[par] = tod
        xb_free = tev
        sf_free = tev
